# version 23
# baseline (speedup 1.0000x reference)
import contextlib
import numpy as np
import concourse.bass as bass
import concourse.mybir as mybir
from concourse.bass_utils import run_bass_kernel_spmd

F32 = mybir.dt.float32
BF16 = mybir.dt.bfloat16
AF = mybir.ActivationFunctionType
ALU = mybir.AluOpType
AX = mybir.AxisListType

P = 128
CT = 8
C = 1024
T = 2048
NT = 16
H = 2048
E = 8
NCORES = 8

ENGINES = ("sync", "gpsimd", "tensor", "vector", "scalar")


class Sem:
    __slots__ = ("h", "n", "name")

    def __init__(self, h, name):
        self.h = h
        self.n = 0
        self.name = name


class Plan:
    def __init__(self):
        self.ev = {e: [] for e in ENGINES}

    def add(self, eng, fn, waits=(), inc=None, inck=1):
        ws = [(s, c) for (s, c) in waits if s is not None and c > 0]
        self.ev[eng].append((ws, fn, inc, inck))
        if inc is not None:
            inc.n += inck
        return inc.n if inc is not None else 0


def _emit(plan, blk):
    for ename in ENGINES:
        events = plan.ev[ename]
        if not events:
            continue

        def make_body(events, ename=ename):
            def body(eng):
                hi = {}
                for ws, fn, inc, inck in events:
                    for sem, cnt in ws:
                        if cnt > hi.get(sem.name, 0):
                            eng.wait_ge(sem.h, cnt)
                            hi[sem.name] = cnt
                    instr = fn(eng)
                    if inc is not None:
                        instr.then_inc(inc.h, inck)
            return body

        getattr(blk, ename)(make_body(events))


def build_program(dbg=None):
    nc = bass.Bass("TRN2", target_bir_lowering=False, debug=False)

    d_xs = nc.dram_tensor("xs", [T, C], F32, kind="ExternalInput").ap()
    d_val = nc.dram_tensor("w_val", [C, C], F32, kind="ExternalInput").ap()
    d_rec = nc.dram_tensor("w_rec", [C, C], F32, kind="ExternalInput").ap()
    d_out = nc.dram_tensor("w_out", [C, C], F32, kind="ExternalInput").ap()
    d_sw1 = nc.dram_tensor("w_sw1", [C, H], F32, kind="ExternalInput").ap()
    d_sw3 = nc.dram_tensor("w_sw3", [C, H], F32, kind="ExternalInput").ap()
    d_sw2 = nc.dram_tensor("w_sw2", [H, C], F32, kind="ExternalInput").ap()
    d_ew1 = nc.dram_tensor("w_ew1", [E, C, C], F32, kind="ExternalInput").ap()
    d_ew2 = nc.dram_tensor("w_ew2", [E, C, C], F32, kind="ExternalInput").ap()
    d_rw = nc.dram_tensor("w_router", [E, C], F32, kind="ExternalInput").ap()
    d_rb = nc.dram_tensor("b_router", [E, 1], F32, kind="ExternalInput").ap()
    d_td = nc.dram_tensor("td2d", [P, CT], F32, kind="ExternalInput").ap()
    d_tf = nc.dram_tensor("tf2d", [P, CT], F32, kind="ExternalInput").ap()
    d_iden = nc.dram_tensor("iden", [P, P], F32, kind="ExternalInput").ap()
    d_oh8 = nc.dram_tensor("oh8", [E, E * P], F32, kind="ExternalInput").ap()
    d_mask = nc.dram_tensor("mask2d", [P, NCORES], F32, kind="ExternalInput").ap()
    d_expo = nc.dram_tensor("expo2d", [P, NCORES], F32, kind="ExternalInput").ap()
    d_ox = nc.dram_tensor("out_x", [T, C], F32, kind="ExternalOutput").ap()
    cc_in = nc.dram_tensor("cc_in", [CT, P], F32, kind="Internal")
    cc_out = nc.dram_tensor("cc_out", [NCORES * CT, P], F32, kind="Internal")
    d_dbg = None
    if dbg is not None:
        d_dbg = nc.dram_tensor("dbg", [P, CT * T], F32, kind="ExternalOutput").ap()

    with contextlib.ExitStack() as stack:
        en = stack.enter_context

        XR = en(nc.sbuf_tensor("XR", [P, CT * T], F32))
        XN = en(nc.sbuf_tensor("XN", [P, CT * T], BF16))
        AR1 = en(nc.sbuf_tensor("AR1", [P, CT * T], BF16))
        AR2 = en(nc.sbuf_tensor("AR2", [P, 18452], F32))
        CV = en(nc.sbuf_tensor("CV", [P, 128], F32))
        CZ = en(nc.sbuf_tensor("CZ", [P, 112], F32))
        IDF = en(nc.sbuf_tensor("IDF", [P, P], F32))
        IDB = en(nc.sbuf_tensor("IDB", [P, P], BF16))
        RWB = en(nc.sbuf_tensor("RWB", [E, C], BF16))
        WRT = en(nc.sbuf_tensor("WRT", [P, CT * E], BF16))
        RBIA = en(nc.sbuf_tensor("RBIA", [E, 1], F32))
        ONEC = en(nc.sbuf_tensor("ONEC", [P, 1], F32))
        EPSB = en(nc.sbuf_tensor("EPSB", [1, 1], F32))
        ONERF = en(nc.sbuf_tensor("ONERF", [1, P], F32))
        ONERB = en(nc.sbuf_tensor("ONERB", [1, P], BF16))
        OH8 = en(nc.sbuf_tensor("OH8", [E, E * P], BF16))

        PT0 = en(nc.psum_tensor("PT0", [P, 2048], F32))
        PT1 = en(nc.psum_tensor("PT1", [P, 2048], F32))
        PS = [PT0[:, i * 512:(i + 1) * 512] for i in range(4)] + \
             [PT1[:, i * 512:(i + 1) * 512] for i in range(4)]

        sems = {}
        for name in ("sIN", "sWG", "sSD", "sT", "sV", "sS", "sG",
                     "sXA", "sXB", "sW1A", "sW1B", "sW3A", "sW3B", "sSW2",
                     "sE1A", "sE1B", "sE2", "sWS"):
            sems[name] = Sem(en(nc.semaphore(name)), name)
        SIN, SWG, SSD = sems["sIN"], sems["sWG"], sems["sSD"]
        ST, SV, SS, SG = sems["sT"], sems["sV"], sems["sS"], sems["sG"]

        pl = Plan()

        A2 = AR2[:]
        A2B = A2.bitcast(BF16)
        AR1F = AR1[:].bitcast(F32)

        def fr(off, n):
            return A2[:, off:off + n]

        def br(off, n):
            return A2B[:, off:off + n]

        def xr(k):
            return XR[:, k * T:(k + 1) * T]

        def xn(k):
            return XN[:, k * T:(k + 1) * T]

        def ar1(k):
            return AR1[:, k * T:(k + 1) * T]

        def cv(i):
            return CV[:, i * 8:(i + 1) * 8]
        TD, TF = cv(0), cv(1)
        RELUW, ALPHA, BETA = cv(2), cv(3), cv(4)
        GAMMA, DELTAEPS, DOB, N2RW = cv(5), cv(6), cv(7), cv(8)
        MASK2, EXPO2 = cv(9), cv(10)
        SCR8A, SCR8B = cv(11), cv(12)
        ZB = CZ[:, 0:8]
        CDAT = CZ[:, 8:16]
        CBB = CZ[:, 16:24]
        ZG = CZ[:, 24:88]
        ZS8 = CZ[:, 88:96]
        CARC = CZ[:, 96:97]
        CARB = CZ[:, 97:98]
        GBDA = CZ[:, 104:112]

        bank_gate = [None] * 8
        t_xr = [0] * CT
        t_xn = [0] * CT
        ag = {"T": 0, "V": 0, "S": 0}

        def agw():
            return [(ST, ag["T"]), (SV, ag["V"]), (SS, ag["S"])]

        def set_ag():
            ag["T"], ag["V"], ag["S"] = ST.n, SV.n, SS.n

        for dst, src in ((CV[:, 0:8], d_td), (CV[:, 8:16], d_tf),
                         (CV[:, 72:80], d_mask), (CV[:, 80:88], d_expo),
                         (IDF[:], d_iden), (RBIA[:], d_rb)):
            pl.add("sync", lambda e, o=dst, i=src: e.dma_start(out=o, in_=i), inc=SIN, inck=16)
        t_consts = SIN.n
        pl.add("gpsimd", lambda e: e.dma_start(out=IDB[:], in_=d_iden), inc=SWG, inck=16)
        pl.add("gpsimd", lambda e: e.dma_start(out=RWB[:], in_=d_rw), inc=SWG, inck=16)
        pl.add("gpsimd", lambda e: e.dma_start(out=OH8[:], in_=d_oh8), inc=SWG, inck=16)
        t_wg0 = SWG.n
        pl.add("vector", lambda e: e.memset(ONEC[:], 1.0), inc=SV)
        pl.add("vector", lambda e: e.memset(ONERF[:], 1.0), inc=SV)
        pl.add("vector", lambda e: e.memset(ONERB[:], 1.0), inc=SV)
        pl.add("vector", lambda e: e.memset(EPSB[:], 1e-5), inc=SV)
        t_ones = SV.n

        RELUU = cv(13)
        DELTA = cv(14)
        GB = cv(15)

        def _cc(e):
            e.tensor_scalar(RELUW, TD, 0.0, None, op0=ALU.max)
            e.tensor_scalar(SCR8A, TD, 0.0, None, op0=ALU.min)
            e.tensor_scalar(SCR8B, TF, 0.0, None, op0=ALU.min)
            e.tensor_scalar(RELUU, TF, 0.0, None, op0=ALU.max)
            e.drain()
            e.tensor_tensor(DOB, SCR8B, SCR8A, op=ALU.subtract)
            return e.tensor_scalar(N2RW, RELUW, -2048.0, None, op0=ALU.mult)
        pl.add("vector", _cc, waits=[(SIN, t_consts)], inc=SV)
        t_ccv = SV.n

        def _cs(e):
            e.activation(ALPHA, RELUW, AF.Exp, scale=-1.0)
            e.activation(BETA, SCR8A, AF.Exp)
            e.activation(DOB, DOB, AF.Exp)
            e.activation(GAMMA, RELUU, AF.Exp, scale=-1.0)
            return e.activation(DELTA, SCR8B, AF.Exp)
        pl.add("scalar", _cs, waits=[(SV, t_ccv)], inc=SS)
        pl.add("scalar", lambda e: e.activation(SCR8A, N2RW, AF.Exp),
               waits=[(SS, SS.n), (SV, t_ccv)], inc=SS)

        def _cv2(e):
            e.tensor_scalar(DELTAEPS, DELTA, 1e-6, None, op0=ALU.add)
            e.tensor_tensor(GB, GAMMA, BETA, op=ALU.mult)
            e.drain()
            e.tensor_tensor(SCR8A, DELTA, ALPHA, op=ALU.mult)
            e.drain()
            e.tensor_tensor(GBDA, GB, SCR8A, op=ALU.subtract)
            e.drain()
            e.tensor_scalar(SCR8B, ALPHA, -1.0, 1.0, op0=ALU.mult, op1=ALU.add)
            e.tensor_scalar(SCR8A, SCR8A, -1.0, 1.0, op0=ALU.mult, op1=ALU.add)
            e.drain()
            e.tensor_single_scalar(ZB, SCR8B, 0.0, op=ALU.is_gt)
            e.drain()
            e.tensor_scalar(CARC8 := CZ[:, 96:104], ZB, -1.0, 1.0, op0=ALU.mult, op1=ALU.add)
            e.drain()
            e.tensor_tensor(SCR8B, SCR8B, CZ[:, 96:104], op=ALU.add)
            e.drain()
            e.reciprocal(CZ[:, 96:104], SCR8B)
            e.drain()
            e.tensor_tensor(SCR8A, SCR8A, CZ[:, 96:104], op=ALU.mult)
            e.drain()
            e.tensor_tensor(SCR8A, SCR8A, ZB, op=ALU.mult)
            e.drain()
            e.tensor_scalar(SCR8B, ZB, -float(T), float(T), op0=ALU.mult, op1=ALU.add)
            e.drain()
            e.tensor_tensor(ZB, SCR8A, SCR8B, op=ALU.add)
            return e.drain()
        pl.add("vector", _cv2, waits=[(SS, SS.n)], inc=SV)
        t_consts_done = SS.n
        t_consts_done_v = SV.n

        xstg = [fr(0, 4096), fr(4096, 4096)]
        t_xtile = [0] * NT
        sX = [sems["sXA"], sems["sXB"]]
        gateX = [None, None]
        for g in range(4):
            gpar = g % 2
            w = [gateX[gpar]] if gateX[gpar] else []
            pl.add("gpsimd", lambda e, o=xstg[gpar], g=g:
                   e.dma_start(out=o.rearrange("p (t c) -> p t c", c=C),
                               in_=d_xs[g * 512:(g + 1) * 512, :]
                               .rearrange("(t p) c -> p t c", p=P)),
                   waits=w, inc=sX[gpar], inck=16)
            xtok = (sX[gpar], sX[gpar].n)
            for tl in range(4):
                tt = g * 4 + tl
                par = tt % 2
                tok = xtok
                bsel = (2 * par, 2 * par + 1)
                wv = [tok, (SIN, t_consts)]
                for b in bsel:
                    if bank_gate[b]:
                        wv.append(bank_gate[b])

                def _tp(e, gpar=gpar, tl=tl, bsel=bsel):
                    r = None
                    for k in range(CT):
                        b = bsel[k // 4]
                        r = e.transpose(PS[b][:, (k % 4) * 128:(k % 4) * 128 + 128],
                                        xstg[gpar][:, tl * 1024 + k * 128: tl * 1024 + k * 128 + 128],
                                        IDF[:])
                    return r
                pl.add("tensor", _tp, waits=wv, inc=ST)
                t_tp = ST.n

                def _ev(e, tt=tt, bsel=bsel):
                    r = None
                    for kk in range(2):
                        for k in range(4):
                            r = e.tensor_copy(xr(kk * 4 + k)[:, tt * 128:(tt + 1) * 128],
                                              PS[bsel[kk]][:, k * 128:k * 128 + 128])
                    return r
                pl.add("vector", _ev, waits=[(ST, t_tp)], inc=SV)
                t_xtile[tt] = SV.n
                for b in bsel:
                    bank_gate[b] = (SV, SV.n)
            gateX[gpar] = (SV, SV.n)
        for k in range(CT):
            t_xr[k] = SV.n

        wv = [(SWG, t_wg0)]
        if bank_gate[5]:
            wv.append(bank_gate[5])

        def _wrt(e):
            r = None
            for kt in range(CT):
                r = e.transpose(PS[5][:, kt * 4:kt * 4 + 4].bitcast(BF16),
                                RWB[:, kt * 128:(kt + 1) * 128], IDB[0:8, 0:8])
            return r
        pl.add("tensor", _wrt, waits=wv, inc=ST)
        pl.add("vector", lambda e: e.tensor_copy(WRT[:], PS[5][:, 0:32].bitcast(BF16)),
               waits=[(ST, ST.n)], inc=SV)
        bank_gate[5] = (SV, SV.n)
        set_ag()

        def plan_layernorm(sq_gate_in=None, xr_gate=None):
            sq = fr(16400, 512)
            nmrow = A2[0:1, 16912:17424]
            rsrow = A2[0:1, 17424:17936]
            srow = A2[0:1, 17936:18448]
            sg = agw() if sq_gate_in is None else list(sq_gate_in)
            for b in range(4):
                cols = slice(b * 512, (b + 1) * 512)
                t_prev_mm = 0
                for k in range(CT):
                    xg = [(SV, t_xr[k])] if xr_gate is None else list(xr_gate(k, b))
                    pl.add("scalar", lambda e, k=k, cols=cols:
                           e.activation(sq[:, 0:512], xr(k)[:, cols], AF.Square),
                           waits=xg + [(ST, t_prev_mm)] + sg, inc=SS)
                    wv = [(SS, SS.n)]
                    if k == 0:
                        for bb_ in (6, 7):
                            if bank_gate[bb_]:
                                wv.append(bank_gate[bb_])
                        wv.append((SV, t_ones))

                    def _mm(e, k=k, cols=cols):
                        e.matmul(PS[6][0:1], ONEC[:], xr(k)[:, cols],
                                 start=(k == 0), stop=(k == CT - 1))
                        return e.matmul(PS[7][0:1], ONEC[:], sq[:, 0:512],
                                        start=(k == 0), stop=(k == CT - 1))
                    pl.add("tensor", _mm, waits=wv, inc=ST)
                    t_prev_mm = ST.n
                t_stats = ST.n

                def _rows_v(e):
                    e.tensor_scalar(nmrow, PS[6][0:1], -1.0 / C, None, op0=ALU.mult)
                    e.drain()
                    e.tensor_tensor(rsrow, nmrow, nmrow, op=ALU.mult)
                    e.drain()
                    e.scalar_tensor_tensor(rsrow, rsrow, -float(C), PS[7][0:1],
                                           op0=ALU.mult, op1=ALU.add)
                    return e.drain()
                pl.add("vector", _rows_v, waits=[(ST, t_stats)] + sg, inc=SV)
                bank_gate[6] = (SV, SV.n)
                bank_gate[7] = (SV, SV.n)
                def _rstd(e):
                    e.activation(srow, rsrow, AF.Ln, scale=1.0 / C, bias=EPSB[0:1, 0:1])
                    e.drain()
                    return e.activation(rsrow, srow, AF.Exp, scale=-0.5)
                pl.add("scalar", _rstd, waits=[(SV, SV.n)], inc=SS)
                t_rows = SS.n

                wv = [(SS, t_rows)]
                for bb_ in (4, 5):
                    if bank_gate[bb_]:
                        wv.append(bank_gate[bb_])

                def _bc(e):
                    e.matmul(PS[4], ONERF[:], nmrow, start=True, stop=True)
                    return e.matmul(PS[5], ONERF[:], rsrow, start=True, stop=True)
                pl.add("tensor", _bc, waits=wv, inc=ST)
                t_bc = ST.n

                for k in range(CT):
                    def _nrm(e, k=k, cols=cols):
                        tmpr = A2[:, 12288 + (k % 2) * 512: 12288 + (k % 2) * 512 + 512]
                        e.tensor_tensor(tmpr, xr(k)[:, cols], PS[4], op=ALU.add)
                        e.drain()
                        return e.tensor_tensor(xn(k)[:, cols], tmpr, PS[5], op=ALU.mult)
                    pl.add("vector", _nrm, waits=[(ST, t_bc)], inc=SV)
                    t_xn[k] = SV.n
                bank_gate[4] = (SV, SV.n)
                bank_gate[5] = (SV, SV.n)

        def plan_router(entry_gate=()):
            s_sb = AR1F[0:8, 0:2048]
            sel_sb = AR1F[0:8, 2048:4096]
            comb_sb = AR1F[0:8, 4096:6144]
            selT = AR1F[:, 6144:6272]
            sT_ = AR1F[:, 6272:6400]
            combT = AR1F[:, 6400:6528]
            m1 = AR1F[:, 6528:6544]
            m2 = AR1F[:, 6544:6560]
            s1 = AR1F[:, 6560:6576]
            s2 = AR1F[:, 6576:6592]
            den = AR1F[:, 6592:6608]
            rec = AR1F[:, 6608:6624]
            cond = AR1F[:, 6624:6640]
            g1 = AR1F[:, 6640:6656]
            g2 = AR1F[:, 6656:6672]
            denp = AR1F[:, 6672:6688]
            tmp3 = AR1F[:, 6784:6912]
            comb_bf = A2B[0:8, 32800:34848]

            for nb in range(4):
                for kt in range(CT):
                    wv = [(SWG, t_wg0), (SV, t_xn[kt])] + agw()
                    if kt == 0 and bank_gate[nb]:
                        wv.append(bank_gate[nb])
                    pl.add("tensor", lambda e, nb=nb, kt=kt: e.matmul(
                        PS[nb][0:8], WRT[:, kt * 8:(kt + 1) * 8],
                        xn(kt)[:, nb * 512:(nb + 1) * 512],
                        start=(kt == 0), stop=(kt == CT - 1)), waits=wv, inc=ST)
            t_lg = ST.n

            def _sg(e):
                r = None
                for nb in range(4):
                    r = e.activation(s_sb[:, nb * 512:(nb + 1) * 512], PS[nb][0:8], AF.Sigmoid)
                return r
            pl.add("scalar", _sg, waits=[(ST, t_lg)] + agw(), inc=SS)
            for nb in range(4):
                bank_gate[nb] = (SS, SS.n)
            pl.add("vector", lambda e: e.tensor_scalar(sel_sb, s_sb, RBIA[:, 0:1], None, op0=ALU.add),
                   waits=[(SS, SS.n), (SIN, t_consts)] + agw() + list(entry_gate), inc=SV)
            t_sel = SV.n

            wv = [(SV, t_sel)]
            if bank_gate[4]:
                wv.append(bank_gate[4])

            def _tps(e):
                r = None
                for tt in range(NT):
                    r = e.transpose(PS[4][:, tt * 8:(tt + 1) * 8],
                                    sel_sb[:, tt * 128:(tt + 1) * 128], IDF[0:8, 0:8])
                return r
            pl.add("tensor", _tps, waits=wv, inc=ST)
            wv = [(SV, t_sel)]
            if bank_gate[5]:
                wv.append(bank_gate[5])

            def _tps2(e):
                r = None
                for tt in range(NT):
                    r = e.transpose(PS[5][:, tt * 8:(tt + 1) * 8],
                                    s_sb[:, tt * 128:(tt + 1) * 128], IDF[0:8, 0:8])
                return r
            pl.add("tensor", _tps2, waits=wv, inc=ST)
            t_tp = ST.n

            def _ev(e):
                e.tensor_copy(selT, PS[4][:, 0:128])
                e.tensor_copy(sT_, PS[5][:, 0:128])
                return e.drain()
            pl.add("vector", _ev, waits=[(ST, t_tp)], inc=SV)
            bank_gate[4] = (SV, SV.n)
            bank_gate[5] = (SV, SV.n)

            sel3 = selT.rearrange("p (g e) -> p g e", e=8)
            s3 = sT_.rearrange("p (g e) -> p g e", e=8)
            comb3 = combT.rearrange("p (g e) -> p g e", e=8)
            tmp33 = tmp3.rearrange("p (g e) -> p g e", e=8)
            m1b = m1.rearrange("p (g o) -> p g o", o=1).broadcast_to((P, 16, 8))
            m2b = m2.rearrange("p (g o) -> p g o", o=1).broadcast_to((P, 16, 8))
            g1b = g1.rearrange("p (g o) -> p g o", o=1).broadcast_to((P, 16, 8))
            g2b = g2.rearrange("p (g o) -> p g o", o=1).broadcast_to((P, 16, 8))

            def _top2(e):
                def d(x):
                    e.drain()
                    return x
                d(e.tensor_reduce(m1, sel3, axis=AX.X, op=ALU.max))
                d(e.tensor_tensor(tmp33, sel3, m1b, op=ALU.is_ge))
                d(e.tensor_tensor(combT, tmp3, sT_, op=ALU.mult))
                d(e.tensor_reduce(s1, combT.rearrange("p (g e) -> p g e", e=8), axis=AX.X, op=ALU.add))
                d(e.scalar_tensor_tensor(combT, tmp3, -1e9, selT, op0=ALU.mult, op1=ALU.add))
                d(e.tensor_reduce(m2, comb3, axis=AX.X, op=ALU.max))
                d(e.tensor_tensor(comb3, comb3, m2b, op=ALU.is_ge))
                d(e.tensor_tensor(sT_, combT, sT_, op=ALU.mult))
                d(e.tensor_reduce(s2, s3, axis=AX.X, op=ALU.add))
                d(e.tensor_tensor(den, s1, s2, op=ALU.add))
                d(e.tensor_scalar(denp, den, 1e-9, None, op0=ALU.add))
                d(e.reciprocal(rec, denp))
                d(e.tensor_single_scalar(cond, den, 1e-9, op=ALU.is_gt))
                d(e.tensor_tensor(g1, s1, rec, op=ALU.mult))
                d(e.tensor_scalar(g1, g1, -0.5, None, op0=ALU.add))
                d(e.tensor_tensor(g1, g1, cond, op=ALU.mult))
                d(e.tensor_scalar(g1, g1, 0.5, 0.25, op0=ALU.mult, op1=ALU.add))
                d(e.tensor_tensor(g2, s2, rec, op=ALU.mult))
                d(e.tensor_scalar(g2, g2, -0.5, None, op0=ALU.add))
                d(e.tensor_tensor(g2, g2, cond, op=ALU.mult))
                d(e.tensor_scalar(g2, g2, 0.5, 0.25, op0=ALU.mult, op1=ALU.add))
                d(e.tensor_tensor(tmp33, tmp33, g1b, op=ALU.mult))
                d(e.tensor_tensor(comb3, comb3, g2b, op=ALU.mult))
                return e.tensor_tensor(combT, combT, tmp3, op=ALU.add)
            pl.add("vector", _top2, waits=[(SV, SV.n)], inc=SV)
            t_top2 = SV.n

            for grp in range(4):
                wv = [(SV, t_top2)]
                if bank_gate[5]:
                    wv.append(bank_gate[5])

                def _tb(e, grp=grp):
                    r = None
                    for t4 in range(4):
                        tt = grp * 4 + t4
                        r = e.transpose(PS[5][0:8, t4 * 128:(t4 + 1) * 128],
                                        combT[:, tt * 8:(tt + 1) * 8], IDF[:])
                    return r
                pl.add("tensor", _tb, waits=wv, inc=ST)
                pl.add("vector", lambda e, grp=grp: e.tensor_copy(
                    comb_sb[:, grp * 512:(grp + 1) * 512], PS[5][0:8]),
                    waits=[(ST, ST.n)], inc=SV)
                bank_gate[5] = (SV, SV.n)
            pl.add("vector", lambda e: (e.drain(), e.tensor_copy(comb_bf, comb_sb))[1],
                   waits=[(SV, SV.n)], inc=SV)
            return comb_bf, SV.n

        sw2_gate = [None]
        sq_gate = [(ST, 0)]
        ln_sq_gate = [None]
        ln_xr_gate = [None]
        ffn_extra = [None]

        def plan_shared(t_nrm):
            s1b = [br(0, 1024), br(1024, 1024)]
            s3b = [br(2048, 1024), br(3072, 1024)]
            sab = [br(4096, 1024), br(5120, 1024)]
            sw2b = br(16384, 16384)
            sW1 = [sems["sW1A"], sems["sW1B"]]
            sW3 = [sems["sW3A"], sems["sW3B"]]
            gW1 = [None, None]
            gW3 = [None, None]
            w = [sw2_gate[0]] if sw2_gate[0] else []
            pl.add("gpsimd", lambda e: e.dma_start(
                out=sw2b.rearrange("p (kt j) -> p kt j", j=C),
                in_=d_sw2.rearrange("(kt p) j -> p kt j", p=P)),
                waits=w + agw() + [(SV, t_nrm)], inc=sems["sSW2"], inck=16)
            t_sw2 = (sems["sSW2"], sems["sSW2"].n)

            xr_ticks = [0] * CT
            t_ab_last = [0]
            t_g_last = [0]
            t_silu_last = [0]
            for nb in range(2):
                g_ticks = [0] * 16
                for m in range(16):
                    par = m % 2
                    toks = []
                    for (dmat, buf, sS_, gX_) in ((d_sw1, s1b, sW1, gW1),
                                                  (d_sw3, s3b, sW3, gW3)):
                        w = [gX_[par]] if gX_[par] else [sq_gate[0]]
                        pl.add("gpsimd", lambda e, o=buf[par], m=m, dm=dmat:
                               e.dma_start(out=o.rearrange("p (kt j) -> p kt j", j=128),
                                           in_=dm[:, m * 128:(m + 1) * 128]
                                           .rearrange("(kt p) j -> p kt j", p=P)),
                               waits=w + agw(), inc=sS_[par], inck=16)
                        toks.append((sS_[par], sS_[par].n))
                    bsa = (2 * par, 2 * par + 1)
                    bsb = (4 + 2 * par, 5 + 2 * par)
                    for kt in range(CT):
                        wv = list(toks) + [(SV, t_xn[kt])]
                        if kt == 0:
                            for b in bsa + bsb:
                                if bank_gate[b]:
                                    wv.append(bank_gate[b])

                        def _mmk(e, par=par, nb=nb, bsa=bsa, bsb=bsb, kt=kt):
                            r = None
                            for (buf, bs) in ((s1b, bsa), (s3b, bsb)):
                                for nn in range(2):
                                    r = e.matmul(PS[bs[nn]],
                                                 buf[par][:, kt * 128:(kt + 1) * 128],
                                                 xn(kt)[:, nb * 1024 + nn * 512: nb * 1024 + (nn + 1) * 512],
                                                 start=(kt == 0), stop=(kt == CT - 1))
                            return r
                        pl.add("tensor", _mmk, waits=wv, inc=ST)
                    t_mm = ST.n
                    t_ab_last[0] = max(t_ab_last[0], t_mm)

                    def _silu(e, par=par, bsa=bsa):
                        e.activation(sab[par][:, 0:512], PS[bsa[0]], AF.Silu)
                        return e.activation(sab[par][:, 512:1024], PS[bsa[1]], AF.Silu)
                    pl.add("scalar", _silu, waits=[(ST, t_mm)], inc=SS)
                    t_silu = SS.n
                    t_silu_last[0] = max(t_silu_last[0], t_silu)
                    for b in bsa:
                        bank_gate[b] = (SS, t_silu)
                    garena = AR1[:, m * 1024:(m + 1) * 1024]

                    def _g(e, par=par, m=m, bsb=bsb, garena=garena):
                        e.scalar_tensor_tensor(garena[:, 0:512], sab[par][:, 0:512], 0.5,
                                               PS[bsb[0]], op0=ALU.mult, op1=ALU.mult)
                        return e.scalar_tensor_tensor(garena[:, 512:1024], sab[par][:, 512:1024], 0.5,
                                                      PS[bsb[1]], op0=ALU.mult, op1=ALU.mult)
                    pl.add("vector", _g, waits=[(SS, t_silu), (ST, t_mm)], inc=SV)
                    g_ticks[m] = SV.n
                    t_g_last[0] = max(t_g_last[0], SV.n)
                    for b in bsb:
                        bank_gate[b] = (SV, SV.n)
                    gW1[par] = (SV, SV.n)
                    gW3[par] = (SV, SV.n)
                for dm in range(CT):
                    par = dm % 2
                    bso = (2 * par, 2 * par + 1)
                    wv = [t_sw2, (SV, max(g_ticks))]
                    for b in bso:
                        if bank_gate[b]:
                            wv.append(bank_gate[b])

                    def _mo(e, dm=dm, bso=bso):
                        r = None
                        for nn in range(2):
                            for kt in range(16):
                                r = e.matmul(PS[bso[nn]],
                                             sw2b[:, kt * 1024 + dm * 128: kt * 1024 + (dm + 1) * 128],
                                             AR1[:, kt * 1024 + nn * 512: kt * 1024 + (nn + 1) * 512],
                                             start=(kt == 0), stop=(kt == 15))
                        return r
                    pl.add("tensor", _mo, waits=wv, inc=ST)
                    t_mo = ST.n

                    def _acc(e, dm=dm, nb=nb, bso=bso):
                        e.tensor_tensor(xr(dm)[:, nb * 1024:nb * 1024 + 512],
                                        xr(dm)[:, nb * 1024:nb * 1024 + 512], PS[bso[0]][:], op=ALU.add)
                        return e.tensor_tensor(xr(dm)[:, nb * 1024 + 512:(nb + 1) * 1024],
                                               xr(dm)[:, nb * 1024 + 512:(nb + 1) * 1024],
                                               PS[bso[1]][:], op=ALU.add)
                    pl.add("vector", _acc, waits=[(ST, t_mo), (SV, t_xr[dm])], inc=SV)
                    xr_ticks[dm] = max(xr_ticks[dm], SV.n)
                    for b in bso:
                        bank_gate[b] = (SV, SV.n)
            sw2_gate[0] = (ST, ST.n)
            ab_gate = [(ST, t_ab_last[0]), (SV, t_g_last[0]), (SS, t_silu_last[0])]
            return xr_ticks, ab_gate

        e_gates = {"e1": [None, None], "e2": [None]}

        def plan_experts(comb_bf, t_comb, gate_e1, gate_e2, dbg_stop=None):
            exp_ticks = {"hp": 0, "cb_mm": 0}
            e_gates["e1"] = [None, None]
            e_gates["e2"] = [None]
            e1b = [br(0, 8192), br(8192, 8192)]
            e2b = br(16384, 8192)
            cbb = [br(24576, 2048), br(26624, 2048)]
            hbuf = AR1[:, 0:8192]
            hpbuf = AR1[:, 8192:16384]
            sE1 = [sems["sE1A"], sems["sE1B"]]
            sE2 = sems["sE2"]
            xr_ticks = [0] * CT

            for ee in range(E):
                par = ee % 2
                w = [e_gates["e1"][par]] if e_gates["e1"][par] else list(gate_e1)
                pl.add("gpsimd", lambda e, o=e1b[par], ee=ee: e.dma_start(
                    out=o.rearrange("p (kt j) -> p kt j", j=C),
                    in_=d_ew1[ee].rearrange("(kt p) j -> p kt j", p=P)),
                    waits=w + agw(), inc=sE1[par], inck=16)
                t_e1 = (sE1[par], sE1[par].n)
                w = [e_gates["e2"][0]] if e_gates["e2"][0] else list(gate_e2)
                pl.add("gpsimd", lambda e, ee=ee: e.dma_start(
                    out=e2b.rearrange("p (kt j) -> p kt j", j=C),
                    in_=d_ew2[ee].rearrange("(kt p) j -> p kt j", p=P)),
                    waits=w + agw(), inc=sE2, inck=16)
                t_e2 = (sE2, sE2.n)

                cpar = ee % 2
                cbank = 6 + cpar
                wv = [(SV, t_comb)]
                for nb4 in range(4):
                    wv2 = list(wv)
                    if bank_gate[cbank]:
                        wv2.append(bank_gate[cbank])

                    def _cb1(e, ee=ee, cbank=cbank, nb4=nb4):
                        return e.matmul(PS[cbank], OH8[:, ee * P:(ee + 1) * P],
                                        comb_bf[0:8, nb4 * 512:(nb4 + 1) * 512],
                                        start=True, stop=True)
                    pl.add("tensor", _cb1, waits=wv2 + [(SWG, t_wg0)], inc=ST)
                    exp_ticks["cb_mm"] = ST.n
                    pl.add("vector", lambda e, cpar=cpar, nb4=nb4, cbank=cbank: e.tensor_copy(
                        cbb[cpar][:, nb4 * 512:(nb4 + 1) * 512], PS[cbank]),
                        waits=[(ST, ST.n)], inc=SV)
                    bank_gate[cbank] = (SV, SV.n)
                t_cb = SV.n

                for nb in range(2):
                    hp_ticks = [0] * CT
                    for hm in range(CT):
                        hpar = hm % 2
                        bsh = (2 * hpar, 2 * hpar + 1)
                        wv = [t_e1, (SV, max(t_xn))]
                        for b in bsh:
                            if bank_gate[b]:
                                wv.append(bank_gate[b])

                        def _hmm(e, hm=hm, nb=nb, par=par, bsh=bsh):
                            r = None
                            for nn in range(2):
                                for kt in range(CT):
                                    r = e.matmul(PS[bsh[nn]],
                                                 e1b[par][:, kt * 1024 + hm * 128: kt * 1024 + (hm + 1) * 128],
                                                 xn(kt)[:, nb * 1024 + nn * 512: nb * 1024 + (nn + 1) * 512],
                                                 start=(kt == 0), stop=(kt == CT - 1))
                            return r
                        pl.add("tensor", _hmm, waits=wv, inc=ST)
                        t_h = ST.n

                        def _gel(e, hm=hm, bsh=bsh):
                            e.activation(hbuf[:, hm * 1024:hm * 1024 + 512], PS[bsh[0]][:], AF.Gelu)
                            return e.activation(hbuf[:, hm * 1024 + 512:(hm + 1) * 1024],
                                                PS[bsh[1]][:], AF.Gelu)
                        gw = [(ST, t_h)]
                        if ee == 0 and nb == 0:
                            gw.append((SV, t_comb))
                        pl.add("scalar", _gel, waits=gw, inc=SS)
                        t_g = SS.n
                        for b in bsh:
                            bank_gate[b] = (SS, t_g)

                        def _hp(e, hm=hm, nb=nb, cpar=cpar):
                            return e.tensor_tensor(hpbuf[:, hm * 1024:(hm + 1) * 1024],
                                                   hbuf[:, hm * 1024:(hm + 1) * 1024],
                                                   cbb[cpar][:, nb * 1024:(nb + 1) * 1024], op=ALU.mult)
                        pl.add("vector", _hp, waits=[(SS, t_g), (SV, t_cb)], inc=SV)
                        hp_ticks[hm] = SV.n
                        exp_ticks["hp"] = SV.n
                    if dbg_stop == "h0":
                        pl.add("vector", lambda e: e.tensor_copy(A2[:, 0:16384], AR1[:]),
                               waits=[(SV, max(hp_ticks))], inc=SV)
                        return ("h0", SV.n)
                    for dm in range(CT):
                        ypar = dm % 2
                        bsy = (4 + 2 * ypar, 5 + 2 * ypar)
                        wv = [t_e2, (SV, max(hp_ticks))]
                        for b in bsy:
                            if bank_gate[b]:
                                wv.append(bank_gate[b])

                        def _ymm(e, dm=dm, bsy=bsy):
                            r = None
                            for nn in range(2):
                                for kt in range(CT):
                                    r = e.matmul(PS[bsy[nn]],
                                                 e2b[:, kt * 1024 + dm * 128: kt * 1024 + (dm + 1) * 128],
                                                 hpbuf[:, kt * 1024 + nn * 512: kt * 1024 + (nn + 1) * 512],
                                                 start=(kt == 0), stop=(kt == CT - 1))
                            return r
                        pl.add("tensor", _ymm, waits=wv, inc=ST)
                        t_y = ST.n

                        def _acc(e, dm=dm, nb=nb, bsy=bsy):
                            e.tensor_tensor(xr(dm)[:, nb * 1024:nb * 1024 + 512],
                                            xr(dm)[:, nb * 1024:nb * 1024 + 512], PS[bsy[0]][:], op=ALU.add)
                            return e.tensor_tensor(xr(dm)[:, nb * 1024 + 512:(nb + 1) * 1024],
                                                   xr(dm)[:, nb * 1024 + 512:(nb + 1) * 1024],
                                                   PS[bsy[1]][:], op=ALU.add)
                        pl.add("vector", _acc, waits=[(ST, t_y), (SV, t_xr[dm])], inc=SV)
                        xr_ticks[dm] = max(xr_ticks[dm], SV.n)
                        for b in bsy:
                            bank_gate[b] = (SV, SV.n)
                e_gates["e1"][par] = (ST, ST.n)
                e_gates["e2"][0] = (ST, ST.n)
            return xr_ticks, exp_ticks

        def plan_ffn(t_ready):
            for k in range(CT):
                t_xr[k] = max(t_xr[k], t_ready[k])
            plan_layernorm(sq_gate_in=ln_sq_gate[0], xr_gate=ln_xr_gate[0])
            ln_sq_gate[0] = None
            ln_xr_gate[0] = None
            sq_gate[0] = (ST, ST.n)
            tick1, ab_gate = plan_shared(max(t_xn))
            t_sw2_end = ST.n
            comb_bf, t_comb = plan_router(entry_gate=[(ST, t_sw2_end)])
            gate_e2 = [(ST, ST.n), (SV, SV.n), (SS, SS.n)]
            tick2, exp_ticks = plan_experts(comb_bf, t_comb, ab_gate, gate_e2)
            ffn_extra[0] = {"sw2_end": t_sw2_end, "hp": exp_ticks["hp"],
                            "cb_mm": exp_ticks["cb_mm"], "comb_v": t_comb}
            return [max(a, b) for a, b in zip(tick1, tick2)]

        def plan_attn(t_ready):
            for k in range(CT):
                t_xr[k] = max(t_xr[k], t_ready[k])
            fx = ffn_extra[0]
            plan_layernorm(sq_gate_in=[(ST, fx["cb_mm"]), (SV, fx["comb_v"])])

            wvalT = br(0, 8192)
            wrecT = br(8192, 8192)
            woutT = br(16384, 8192)
            wstg = br(24576, 8192)
            sWS = sems["sWS"]
            wT_ticks = {}
            gstg = None

            def prep_weight(dmat, dstT, extra_dma_waits=(), evac_waits=()):
                nonlocal gstg
                pl.add("gpsimd", lambda e, dm=dmat: e.dma_start(
                    out=wstg.rearrange("p (dt j) -> p dt j", j=C),
                    in_=dm.rearrange("(dt p) j -> p dt j", p=P)),
                    waits=list(gstg if gstg is not None else agw()) + list(extra_dma_waits),
                    inc=sWS, inck=16)
                t_l = (sWS, sWS.n)
                for kt in range(CT):
                    par = kt % 2
                    bank = 6 + par
                    wv = [t_l, (SWG, t_wg0)]
                    if bank_gate[bank]:
                        wv.append(bank_gate[bank])

                    def _tpw(e, kt=kt, bank=bank):
                        r = None
                        for dt in range(CT):
                            r = e.transpose(PS[bank][:, dt * 64:dt * 64 + 64].bitcast(BF16),
                                            wstg[:, dt * 1024 + kt * 128: dt * 1024 + (kt + 1) * 128],
                                            IDB[:])
                        return r
                    pl.add("tensor", _tpw, waits=wv, inc=ST)
                    pl.add("vector", lambda e, kt=kt, bank=bank, dstT=dstT: e.tensor_copy(
                        dstT[:, kt * 1024:(kt + 1) * 1024], PS[bank][:, 0:512].bitcast(BF16)),
                        waits=[(ST, ST.n)] + list(evac_waits), inc=SV)
                    bank_gate[bank] = (SV, SV.n)
                gstg = [(ST, ST.n)]
                return SV.n

            sq_gate[0] = (ST, ST.n)
            t_nrm_done = SV.n
            early = [(ST, fx["sw2_end"]), (SV, max(fx["hp"], t_nrm_done))]
            gstg = early
            wT_ticks["val"] = prep_weight(d_val, wvalT, evac_waits=[sq_gate[0]])
            wT_ticks["rec"] = prep_weight(d_rec, wrecT)

            zA = fr(12288, 2048)
            zO = fr(14344, 2048)

            t_sigr = [0] * CT
            for ct in range(CT):
                for kt in range(CT):
                    wv = [(SV, max(wT_ticks["val"], t_xn[kt]))]
                    if kt == 0:
                        for b in range(4):
                            if bank_gate[b]:
                                wv.append(bank_gate[b])

                    def _vmmk(e, ct=ct, kt=kt):
                        r = None
                        for nb in range(4):
                            r = e.matmul(PS[nb],
                                         wvalT[:, kt * 1024 + ct * 128: kt * 1024 + (ct + 1) * 128],
                                         xn(kt)[:, nb * 512:(nb + 1) * 512],
                                         start=(kt == 0), stop=(kt == CT - 1))
                        return r
                    pl.add("tensor", _vmmk, waits=wv, inc=ST)
                t_v = ST.n
                wv = [(SV, wT_ticks["rec"])]
                for b in range(4, 8):
                    if bank_gate[b]:
                        wv.append(bank_gate[b])

                def _rmm(e, ct=ct):
                    r = None
                    for nb in range(4):
                        for kt in range(CT):
                            r = e.matmul(PS[4 + nb],
                                         wrecT[:, kt * 1024 + ct * 128: kt * 1024 + (ct + 1) * 128],
                                         xn(kt)[:, nb * 512:(nb + 1) * 512],
                                         start=(kt == 0), stop=(kt == CT - 1))
                    return r
                pl.add("tensor", _rmm, waits=wv, inc=ST)
                t_r = ST.n

                albcT = ALPHA[:, ct:ct + 1].broadcast_to((P, T))
                pl.add("vector", lambda e, albcT=albcT: (e.tensor_tensor_scan(
                    zO[:, 0:T], albcT, PT0[:], 0.0, op0=ALU.mult, op1=ALU.add), e.drain())[1],
                    waits=[(ST, t_v), (SS, t_consts_done), (SV, t_ones)], inc=SV)
                for b in range(4):
                    bank_gate[b] = (SV, SV.n)
                pl.add("vector", lambda e, ct=ct: e.tensor_copy(ZS8[:, ct:ct + 1], zO[:, T - 1:T]),
                       waits=[(SV, SV.n)], inc=SV)
                t_zc = SV.n
                pl.add("gpsimd", lambda e, ct=ct: e.dma_start(
                    out=cc_in.ap()[ct:ct + 1, :].rearrange("o p -> p o"), in_=ZS8[:, ct:ct + 1]),
                    waits=[(SV, t_zc)], inc=SWG, inck=16)


                def _sig(e, ct=ct):
                    r = None
                    for nb in range(4):
                        r = e.activation(ar1(ct)[:, nb * 512:(nb + 1) * 512], PS[4 + nb], AF.Sigmoid)
                    return r
                pl.add("scalar", _sig, waits=[(ST, t_r)] + agw(), inc=SS)
                t_sigr[ct] = SS.n
                for b in range(4, 8):
                    bank_gate[b] = (SS, SS.n)
            t_zdma = SWG.n
            t_phA_v = SV.n

            pl.add("gpsimd", lambda e: e.collective_compute(
                "AllGather", ALU.bypass, replica_groups=[list(range(NCORES))],
                ins=[cc_in.ap()], outs=[cc_out.ap()]),
                waits=[(SWG, t_zdma)], inc=SG)
            t_cc = SG.n
            wT_ticks["out"] = prep_weight(d_out, woutT, extra_dma_waits=[(SV, t_phA_v)])
            t_wouttp = ST.n
            for ct in range(CT):
                pl.add("gpsimd", lambda e, ct=ct: e.dma_start(
                    out=ZG[:, ct * 8:(ct + 1) * 8],
                    in_=cc_out.ap().rearrange("(r c) p -> c p r", c=CT)[ct],
                    allow_slow_non_contiguous=True),
                    waits=[(SG, t_cc)], inc=SWG, inck=16)
            t_zg = SWG.n

            AARG = fr(16400, 64)
            APRD = fr(16464, 64)
            ex3 = EXPO2.rearrange("p (o e) -> p o e", o=1).broadcast_to((P, 8, 8))
            mk3 = MASK2.rearrange("p (o e) -> p o e", o=1).broadcast_to((P, 8, 8))
            nw3 = N2RW.rearrange("p (c o) -> p c o", o=1).broadcast_to((P, 8, 8))
            pl.add("vector", lambda e: (e.tensor_tensor(
                AARG.rearrange("p (c e) -> p c e", e=8), ex3, nw3, op=ALU.mult), e.drain())[1],
                waits=[(SV, t_phA_v), (SS, t_consts_done)], inc=SV)
            pl.add("scalar", lambda e: e.activation(AARG, AARG, AF.Exp),
                   waits=[(SV, SV.n)], inc=SS)

            def _combb(e):
                e.tensor_tensor(AARG.rearrange("p (c e) -> p c e", e=8),
                                AARG.rearrange("p (c e) -> p c e", e=8), mk3, op=ALU.mult)
                e.drain()
                e.tensor_tensor(APRD, AARG, ZG[:, 0:64], op=ALU.mult)
                e.drain()
                e.tensor_reduce(CDAT, APRD.rearrange("p (c e) -> p c e", e=8), axis=AX.X, op=ALU.add)
                e.drain()
                e.tensor_reduce(SCR8B, AARG.rearrange("p (c e) -> p c e", e=8), axis=AX.X, op=ALU.add)
                e.drain()
                e.tensor_tensor(CBB, SCR8B, ZB, op=ALU.mult)
                return e.drain()
            pl.add("vector", _combb, waits=[(SS, SS.n), (SWG, t_zg)], inc=SV)
            t_carry = SV.n

            zzf2 = [fr(12288, 2049), fr(16400, 2049)]
            zbf = fr(14344, 2049)
            numf2 = fr(6144, 2048)
            DEN = fr(4096, 2048)
            rw_sg = [0] * CT
            rw_ticks = [0] * CT
            t_dv = [0] * (CT + 1)
            t_scans = [0] * (CT + 1)
            t_den = [0] * (CT + 1)
            t_wkvA = [0] * (CT + 1)
            t_wkvB = [0] * (CT + 1)
            t_vmm = [0] * (CT + 1)

            def plan_vmm(ct):
                wv = [(SV, wT_ticks["val"])]
                if ct > 0:
                    wv += [(SV, t_scans[ct - 1])]
                else:
                    for b in range(4):
                        if bank_gate[b]:
                            wv.append(bank_gate[b])

                def _vmm2(e, ct=ct):
                    r = None
                    for nb in range(4):
                        for kt in range(CT):
                            r = e.matmul(PS[nb],
                                         wvalT[:, kt * 1024 + ct * 128: kt * 1024 + (ct + 1) * 128],
                                         xn(kt)[:, nb * 512:(nb + 1) * 512],
                                         start=(kt == 0), stop=(kt == CT - 1))
                    return r
                pl.add("tensor", _vmm2, waits=wv, inc=ST)
                t_vmm[ct] = ST.n

            def plan_scans(ct):
                p = ct % 2
                albc = ALPHA[:, ct:ct + 1].broadcast_to((P, T))
                onbc = ONEC[:, 0:1].broadcast_to((P, T))

                def _scans(e, ct=ct, p=p, albc=albc, onbc=onbc):
                    e.tensor_copy(zzf2[p][:, 0:1], CDAT[:, ct:ct + 1])
                    e.tensor_copy(zbf[:, 0:1], CBB[:, ct:ct + 1])
                    e.tensor_tensor_scan(zzf2[p][:, 1:T + 1], albc, PT0[:], CDAT[:, ct:ct + 1],
                                         op0=ALU.mult, op1=ALU.add)
                    e.tensor_tensor_scan(zbf[:, 1:T + 1], albc, onbc, CBB[:, ct:ct + 1],
                                         op0=ALU.mult, op1=ALU.add)
                    return e.drain()
                wv = [(ST, max(t_vmm[ct], t_wouttp)), (SV, t_carry)]
                if ct > 0:
                    wv.append((SS, t_den[ct - 1]))
                if ct > 1:
                    wv.append((SG, rw_sg[ct - 2]))
                pl.add("vector", _scans, waits=wv, inc=SV)
                t_scans[ct] = SV.n

            def plan_dv(ct):
                t_dv[ct] = SS.n

            def plan_den(ct):
                def _den(e, ct=ct):
                    e.activation(DEN, zbf[:, 0:T], AF.Relu, scale=GB[:, ct:ct + 1],
                                 bias=DELTAEPS[:, ct:ct + 1])
                    e.drain()
                    e.activation(zbf[:, 0:T], DEN, AF.Ln)
                    e.drain()
                    return e.activation(DEN, zbf[:, 0:T], AF.Exp, scale=-1.0)
                wv = [(SV, t_scans[ct]), (ST, t_wouttp)]
                if ct > 0:
                    wv.append((SV, t_wkvB[ct - 1]))
                pl.add("scalar", _den, waits=wv, inc=SS)
                t_den[ct] = SS.n

            def plan_wkvA(ct):
                p = ct % 2

                def _wkvA(e, ct=ct, p=p):
                    e.tensor_scalar(numf2, zzf2[p][:, 1:T + 1], DELTA[:, ct:ct + 1], None,
                                    op0=ALU.mult)
                    e.drain()
                    e.scalar_tensor_tensor(zzf2[p][:, 0:T], zzf2[p][:, 0:T], GBDA[:, ct:ct + 1],
                                           numf2, op0=ALU.mult, op1=ALU.add)
                    return e.drain()
                pl.add("vector", _wkvA, waits=[(SV, t_consts_done_v)], inc=SV)
                t_wkvA[ct] = SV.n

            def plan_wkvB(ct):
                p = ct % 2

                def _wkvB(e, ct=ct, p=p):
                    e.tensor_tensor(zzf2[p][:, 0:T], zzf2[p][:, 0:T], DEN, op=ALU.mult)
                    return e.drain()
                pl.add("vector", _wkvB,
                       waits=[(SS, max(t_den[ct], t_sigr[ct])), (SV, t_consts_done_v)], inc=SV)
                t_wkvB[ct] = SV.n
                pl.add("gpsimd", lambda e, ct=ct, p=p: e.tensor_tensor(
                    ar1(ct)[:, 0:T], zzf2[p][:, 0:T], ar1(ct)[:, 0:T], op=ALU.mult),
                    waits=[(SV, t_wkvB[ct])], inc=SG)
                rw_sg[ct] = SG.n
                rw_ticks[ct] = SV.n

            def plan_wave(kt):
                wv = [(SG, rw_sg[kt]), (SV, wT_ticks["out"])]
                if kt == 0:
                    for b in range(4, 8):
                        if bank_gate[b]:
                            wv.append(bank_gate[b])

                def _wv(e, kt=kt):
                    r = None
                    for g in range(4):
                        r = e.matmul(PS[4 + g],
                                     woutT[:, kt * 1024 + g * 128: kt * 1024 + (g + 1) * 128],
                                     ar1(kt)[:, 0:512],
                                     start=(kt == 0), stop=(kt == CT - 1))
                    return r
                pl.add("tensor", _wv, waits=wv, inc=ST)

            plan_vmm(0)
            plan_scans(0)
            plan_dv(0)
            plan_den(0)
            for ct in range(1, CT):
                plan_vmm(ct)
                plan_wkvA(ct - 1)
                plan_scans(ct)
                plan_dv(ct)
                plan_wkvB(ct - 1)
                plan_wave(ct - 1)
                plan_den(ct)
            plan_wkvA(CT - 1)
            plan_wkvB(CT - 1)
            plan_wave(CT - 1)
            t_wave = ST.n
            for g in range(4):
                pl.add("vector", lambda e, g=g: e.tensor_tensor(
                    xr(g)[:, 0:512], xr(g)[:, 0:512], PS[4 + g], op=ALU.add),
                    waits=[(ST, t_wave), (SV, t_xr[g])], inc=SV)
                bank_gate[4 + g] = (SV, SV.n)

            xr_ticks = [0] * CT
            for g in range(4):
                xr_ticks[g] = SV.n
            for dm in range(CT):
                for nb in range(4):
                    if nb == 0 and dm < 4:
                        continue
                    par = (dm * 4 + nb) % 2
                    bank = 4 + par
                    wv = [(SG, max(rw_sg)), (SV, wT_ticks["out"])]
                    if bank_gate[bank]:
                        wv.append(bank_gate[bank])
                    wv.append((SV, t_wkvB[CT - 1]))

                    def _omm(e, dm=dm, nb=nb, bank=bank):
                        r = None
                        for kt in range(CT):
                            r = e.matmul(PS[bank],
                                         woutT[:, kt * 1024 + dm * 128: kt * 1024 + (dm + 1) * 128],
                                         ar1(kt)[:, nb * 512:(nb + 1) * 512],
                                         start=(kt == 0), stop=(kt == CT - 1))
                        return r
                    pl.add("tensor", _omm, waits=wv, inc=ST)
                    t_o = ST.n
                    pl.add("vector", lambda e, dm=dm, nb=nb, bank=bank: e.tensor_tensor(
                        xr(dm)[:, nb * 512:(nb + 1) * 512],
                        xr(dm)[:, nb * 512:(nb + 1) * 512], PS[bank], op=ALU.add),
                        waits=[(ST, t_o), (SV, t_xr[dm])], inc=SV)
                    xr_ticks[dm] = SV.n
                    bank_gate[bank] = (SV, SV.n)
            return xr_ticks

        def plan_out(t_ready):
            otok = [fr(0, 1024), fr(1024, 1024)]
            n_st = 0
            gateO = [None, None]
            for tt in range(NT):
                par = tt % 2
                bsel = (2 * par, 2 * par + 1)
                for half in range(2):
                    def _tpo(e, tt=tt, bsel=bsel, half=half):
                        r = None
                        for k in range(4 * half, 4 * half + 4):
                            b = bsel[k // 4]
                            r = e.transpose(PS[b][:, (k % 4) * 128:(k % 4) * 128 + 128],
                                            xr(k)[:, tt * 128:(tt + 1) * 128], IDF[:])
                        return r
                    wvh = [(SV, max(t_ready[4 * half:4 * half + 4]))] + [
                        g for g in ([bank_gate[b] for b in bsel[half:half + 1]]) if g]
                    pl.add("tensor", _tpo, waits=wvh, inc=ST)
                t_tp = ST.n
                wv2 = [(ST, t_tp)]
                if gateO[par]:
                    wv2.append(gateO[par])

                def _evo(e, par=par, bsel=bsel):
                    r = None
                    for kk in range(2):
                        for k in range(4):
                            r = e.tensor_copy(otok[par][:, (kk * 4 + k) * 128:(kk * 4 + k + 1) * 128],
                                              PS[bsel[kk]][:, k * 128:k * 128 + 128])
                    return r
                pl.add("vector", _evo, waits=wv2, inc=SV)
                t_ev = SV.n
                for b in bsel:
                    bank_gate[b] = (SV, t_ev)
                pl.add("scalar", lambda e, tt=tt, par=par: e.dma_start(
                    out=d_ox[tt * P:(tt + 1) * P, :], in_=otok[par][:]),
                    waits=[(SV, t_ev)], inc=SSD, inck=16)
                n_st += 1
                gateO[par] = (SSD, 16 * n_st)
            pl.add("scalar", lambda e: e.nop(), waits=[(SSD, 16 * n_st)])

        def plan_dbg_dump(src, t_ready):
            tmax = max(t_ready) if isinstance(t_ready, list) else t_ready
            pl.add("scalar", lambda e: e.dma_start(out=d_dbg, in_=src),
                   waits=[(SV, tmax), (ST, ST.n), (SS, SS.n)], inc=SSD, inck=16)
            pl.add("scalar", lambda e: e.nop(), waits=[(SSD, SSD.n)])

        ln_sq_gate[0] = [(ST, 0)]
        ln_xr_gate[0] = lambda k, b: [(SV, max(t_xtile[4 * b:4 * b + 4]))]
        if dbg == "xt":
            plan_dbg_dump(XR[:], t_xr)
        elif dbg == "shared":
            plan_layernorm()
            ticks, _abg = plan_shared(max(t_xn))
            plan_dbg_dump(XR[:], ticks)
        elif dbg == "ln1":
            plan_layernorm()
            pl.add("vector", lambda e: e.tensor_copy(AR2[:, 0:CT * T], XN[:]),
                   waits=[(SV, max(t_xn))], inc=SV)
            plan_dbg_dump(AR2[:, 0:CT * T], SV.n)
        else:
            ticks = plan_ffn(t_xr)
            set_ag()
            if dbg == "ffn1":
                plan_dbg_dump(XR[:], ticks)
            else:
                ticks = plan_attn(ticks)
                set_ag()
                if dbg == "attn":
                    plan_dbg_dump(XR[:], ticks)
                else:
                    ticks = plan_ffn(ticks)
                    set_ag()
                    if dbg == "ffn2":
                        plan_dbg_dump(XR[:], ticks)
                    else:
                        plan_out(ticks)

        with nc.Block() as blk:
            _emit(pl, blk)

    return nc


_cache = {}


def _get_nc(dbg=None):
    if dbg not in _cache:
        _cache[dbg] = build_program(dbg)
    return _cache[dbg]


def make_in_maps(inputs):
    x = np.asarray(inputs["L0_attn_x"])
    td = np.asarray(inputs["L0_attn_time_decay"])
    tf = np.asarray(inputs["L0_attn_time_first"])
    td2d = np.ascontiguousarray(td.reshape(CT, P).T)
    tf2d = np.ascontiguousarray(tf.reshape(CT, P).T)
    iden = np.eye(P, dtype=np.float32)
    shared = {
        "w_val": np.ascontiguousarray(inputs["L0_attn_value_w"], dtype=np.float32),
        "w_rec": np.ascontiguousarray(inputs["L0_attn_recep_w"], dtype=np.float32),
        "w_out": np.ascontiguousarray(inputs["L0_attn_out_w"], dtype=np.float32),
        "w_sw1": np.ascontiguousarray(inputs["L0_ffn_sw1"], dtype=np.float32),
        "w_sw3": np.ascontiguousarray(inputs["L0_ffn_sw3"], dtype=np.float32),
        "w_sw2": np.ascontiguousarray(inputs["L0_ffn_sw2"], dtype=np.float32),
        "w_ew1": np.ascontiguousarray(inputs["L0_ffn_ew1"], dtype=np.float32),
        "w_ew2": np.ascontiguousarray(inputs["L0_ffn_ew2"], dtype=np.float32),
        "w_router": np.ascontiguousarray(inputs["L0_ffn_router_w"], dtype=np.float32),
        "b_router": np.ascontiguousarray(
            np.asarray(inputs["L0_ffn_router_bias"], dtype=np.float32).reshape(E, 1)),
        "td2d": td2d, "tf2d": tf2d, "iden": iden,
        "oh8": np.ascontiguousarray(
            np.kron(np.eye(E, dtype=np.float32), np.ones((1, P), np.float32))),
    }
    in_maps = []
    for core in range(NCORES):
        b, q = core // 4, core % 4
        mask = np.zeros(NCORES, np.float32)
        expo = np.zeros(NCORES, np.float32)
        for qp in range(q):
            mask[b * 4 + qp] = 1.0
            expo[b * 4 + qp] = float(q - 1 - qp)
        im = dict(shared)
        im["xs"] = np.ascontiguousarray(x[b, q * T:(q + 1) * T, :], dtype=np.float32)
        im["mask2d"] = np.ascontiguousarray(np.tile(mask, (P, 1)))
        im["expo2d"] = np.ascontiguousarray(np.tile(expo, (P, 1)))
        in_maps.append(im)
    return in_maps


def kernel(**inputs):
    nc = _get_nc(None)
    in_maps = make_in_maps(inputs)
    res = run_bass_kernel_spmd(nc, in_maps, core_ids=list(range(NCORES)))
    out = np.empty((2, 4 * T, C), np.float32)
    for core in range(NCORES):
        b, q = core // 4, core % 4
        out[b, q * T:(q + 1) * T, :] = res.results[core]["out_x"]
    return out


# version 25
# speedup vs baseline: 1.0647x; 1.0647x over previous
import contextlib
import numpy as np
import concourse.bass as bass
import concourse.mybir as mybir
from concourse.bass_utils import run_bass_kernel_spmd

F32 = mybir.dt.float32
BF16 = mybir.dt.bfloat16
AF = mybir.ActivationFunctionType
ALU = mybir.AluOpType
AX = mybir.AxisListType

P = 128
CT = 8
C = 1024
T = 2048
NT = 16
H = 2048
E = 8
NCORES = 8

ENGINES = ("sync", "gpsimd", "tensor", "vector", "scalar")


class Sem:
    __slots__ = ("h", "n", "name")

    def __init__(self, h, name):
        self.h = h
        self.n = 0
        self.name = name


class Plan:
    def __init__(self):
        self.ev = {e: [] for e in ENGINES}

    def add(self, eng, fn, waits=(), inc=None, inck=1):
        ws = [(s, c) for (s, c) in waits if s is not None and c > 0]
        self.ev[eng].append((ws, fn, inc, inck))
        if inc is not None:
            inc.n += inck
        return inc.n if inc is not None else 0


def _emit(plan, blk):
    for ename in ENGINES:
        events = plan.ev[ename]
        if not events:
            continue

        def make_body(events, ename=ename):
            def body(eng):
                hi = {}
                for ws, fn, inc, inck in events:
                    for sem, cnt in ws:
                        if cnt > hi.get(sem.name, 0):
                            eng.wait_ge(sem.h, cnt)
                            hi[sem.name] = cnt
                    instr = fn(eng)
                    if inc is not None:
                        instr.then_inc(inc.h, inck)
            return body

        getattr(blk, ename)(make_body(events))


def build_program(dbg=None):
    nc = bass.Bass("TRN2", target_bir_lowering=False, debug=False)

    d_xs = nc.dram_tensor("xs", [T, C], F32, kind="ExternalInput").ap()
    d_val = nc.dram_tensor("w_val", [C, C], F32, kind="ExternalInput").ap()
    d_rec = nc.dram_tensor("w_rec", [C, C], F32, kind="ExternalInput").ap()
    d_out = nc.dram_tensor("w_out", [C, C], F32, kind="ExternalInput").ap()
    d_sw1 = nc.dram_tensor("w_sw1", [C, H], F32, kind="ExternalInput").ap()
    d_sw3 = nc.dram_tensor("w_sw3", [C, H], F32, kind="ExternalInput").ap()
    d_sw2 = nc.dram_tensor("w_sw2", [H, C], F32, kind="ExternalInput").ap()
    d_ew1 = nc.dram_tensor("w_ew1", [E, C, C], F32, kind="ExternalInput").ap()
    d_ew2 = nc.dram_tensor("w_ew2", [E, C, C], F32, kind="ExternalInput").ap()
    d_rw = nc.dram_tensor("w_router", [E, C], F32, kind="ExternalInput").ap()
    d_rb = nc.dram_tensor("b_router", [E, 1], F32, kind="ExternalInput").ap()
    d_td = nc.dram_tensor("td2d", [P, CT], F32, kind="ExternalInput").ap()
    d_tf = nc.dram_tensor("tf2d", [P, CT], F32, kind="ExternalInput").ap()
    d_iden = nc.dram_tensor("iden", [P, P], F32, kind="ExternalInput").ap()
    d_oh8 = nc.dram_tensor("oh8", [E, E * P], F32, kind="ExternalInput").ap()
    d_mask = nc.dram_tensor("mask2d", [P, NCORES], F32, kind="ExternalInput").ap()
    d_expo = nc.dram_tensor("expo2d", [P, NCORES], F32, kind="ExternalInput").ap()
    d_ox = nc.dram_tensor("out_x", [T, C], F32, kind="ExternalOutput").ap()
    cc_in = nc.dram_tensor("cc_in", [CT, P], F32, kind="Internal")
    cc_out = nc.dram_tensor("cc_out", [NCORES * CT, P], F32, kind="Internal")
    d_dbg = None
    if dbg is not None:
        d_dbg = nc.dram_tensor("dbg", [P, CT * T], F32, kind="ExternalOutput").ap()

    with contextlib.ExitStack() as stack:
        en = stack.enter_context

        XR = en(nc.sbuf_tensor("XR", [P, CT * T], F32))
        XN = en(nc.sbuf_tensor("XN", [P, CT * T], BF16))
        AR1 = en(nc.sbuf_tensor("AR1", [P, CT * T], BF16))
        AR2 = en(nc.sbuf_tensor("AR2", [P, 18452], F32))
        CV = en(nc.sbuf_tensor("CV", [P, 128], F32))
        CZ = en(nc.sbuf_tensor("CZ", [P, 112], F32))
        IDF = en(nc.sbuf_tensor("IDF", [P, P], F32))
        IDB = en(nc.sbuf_tensor("IDB", [P, P], BF16))
        RWB = en(nc.sbuf_tensor("RWB", [E, C], BF16))
        WRT = en(nc.sbuf_tensor("WRT", [P, CT * E], BF16))
        RBIA = en(nc.sbuf_tensor("RBIA", [E, 1], F32))
        ONEC = en(nc.sbuf_tensor("ONEC", [P, 1], F32))
        EPSB = en(nc.sbuf_tensor("EPSB", [1, 1], F32))
        ONERF = en(nc.sbuf_tensor("ONERF", [1, P], F32))
        ONERB = en(nc.sbuf_tensor("ONERB", [1, P], BF16))
        OH8 = en(nc.sbuf_tensor("OH8", [E, E * P], BF16))

        PT0 = en(nc.psum_tensor("PT0", [P, 2048], F32))
        PT1 = en(nc.psum_tensor("PT1", [P, 2048], F32))
        PS = [PT0[:, i * 512:(i + 1) * 512] for i in range(4)] + \
             [PT1[:, i * 512:(i + 1) * 512] for i in range(4)]

        sems = {}
        for name in ("sIN", "sWG", "sSD", "sT", "sV", "sS", "sG",
                     "sXA", "sXB", "sW1A", "sW1B", "sW3A", "sW3B", "sSW2",
                     "sE1A", "sE1B", "sE2", "sWS"):
            sems[name] = Sem(en(nc.semaphore(name)), name)
        SIN, SWG, SSD = sems["sIN"], sems["sWG"], sems["sSD"]
        ST, SV, SS, SG = sems["sT"], sems["sV"], sems["sS"], sems["sG"]

        pl = Plan()

        A2 = AR2[:]
        A2B = A2.bitcast(BF16)
        AR1F = AR1[:].bitcast(F32)

        def fr(off, n):
            return A2[:, off:off + n]

        def br(off, n):
            return A2B[:, off:off + n]

        def xr(k):
            return XR[:, k * T:(k + 1) * T]

        def xn(k):
            return XN[:, k * T:(k + 1) * T]

        def ar1(k):
            return AR1[:, k * T:(k + 1) * T]

        def cv(i):
            return CV[:, i * 8:(i + 1) * 8]
        TD, TF = cv(0), cv(1)
        RELUW, ALPHA, BETA = cv(2), cv(3), cv(4)
        GAMMA, DELTAEPS, DOB, N2RW = cv(5), cv(6), cv(7), cv(8)
        MASK2, EXPO2 = cv(9), cv(10)
        SCR8A, SCR8B = cv(11), cv(12)
        ZB = CZ[:, 0:8]
        CDAT = CZ[:, 8:16]
        CBB = CZ[:, 16:24]
        ZG = CZ[:, 24:88]
        ZS8 = CZ[:, 88:96]
        CARC = CZ[:, 96:97]
        CARB = CZ[:, 97:98]
        GBDA = CZ[:, 104:112]

        bank_gate = [None] * 8
        t_xr = [0] * CT
        t_xn = [0] * CT
        ag = {"T": 0, "V": 0, "S": 0}

        def agw():
            return [(ST, ag["T"]), (SV, ag["V"]), (SS, ag["S"])]

        def set_ag():
            ag["T"], ag["V"], ag["S"] = ST.n, SV.n, SS.n

        for dst, src in ((CV[:, 0:8], d_td), (CV[:, 8:16], d_tf),
                         (CV[:, 72:80], d_mask), (CV[:, 80:88], d_expo),
                         (IDF[:], d_iden), (RBIA[:], d_rb)):
            pl.add("sync", lambda e, o=dst, i=src: e.dma_start(out=o, in_=i), inc=SIN, inck=16)
        t_consts = SIN.n
        pl.add("gpsimd", lambda e: e.dma_start(out=IDB[:], in_=d_iden), inc=SWG, inck=16)
        pl.add("gpsimd", lambda e: e.dma_start(out=RWB[:], in_=d_rw), inc=SWG, inck=16)
        pl.add("gpsimd", lambda e: e.dma_start(out=OH8[:], in_=d_oh8), inc=SWG, inck=16)
        t_wg0 = SWG.n
        pl.add("vector", lambda e: e.memset(ONEC[:], 1.0), inc=SV)
        pl.add("vector", lambda e: e.memset(ONERF[:], 1.0), inc=SV)
        pl.add("vector", lambda e: e.memset(ONERB[:], 1.0), inc=SV)
        pl.add("vector", lambda e: e.memset(EPSB[:], 1e-5), inc=SV)
        t_ones = SV.n

        RELUU = cv(13)
        DELTA = cv(14)
        GB = cv(15)

        def _cc(e):
            e.tensor_scalar(RELUW, TD, 0.0, None, op0=ALU.max)
            e.tensor_scalar(SCR8A, TD, 0.0, None, op0=ALU.min)
            e.tensor_scalar(SCR8B, TF, 0.0, None, op0=ALU.min)
            e.tensor_scalar(RELUU, TF, 0.0, None, op0=ALU.max)
            e.drain()
            e.tensor_tensor(DOB, SCR8B, SCR8A, op=ALU.subtract)
            return e.tensor_scalar(N2RW, RELUW, -2048.0, None, op0=ALU.mult)
        pl.add("vector", _cc, waits=[(SIN, t_consts)], inc=SV)
        t_ccv = SV.n

        def _cs(e):
            e.activation(ALPHA, RELUW, AF.Exp, scale=-1.0)
            e.activation(BETA, SCR8A, AF.Exp)
            e.activation(DOB, DOB, AF.Exp)
            e.activation(GAMMA, RELUU, AF.Exp, scale=-1.0)
            return e.activation(DELTA, SCR8B, AF.Exp)
        pl.add("scalar", _cs, waits=[(SV, t_ccv)], inc=SS)
        pl.add("scalar", lambda e: e.activation(SCR8A, N2RW, AF.Exp),
               waits=[(SS, SS.n), (SV, t_ccv)], inc=SS)

        def _cv2(e):
            e.tensor_scalar(DELTAEPS, DELTA, 1e-6, None, op0=ALU.add)
            e.tensor_tensor(GB, GAMMA, BETA, op=ALU.mult)
            e.drain()
            e.tensor_tensor(SCR8A, DELTA, ALPHA, op=ALU.mult)
            e.drain()
            e.tensor_tensor(GBDA, GB, SCR8A, op=ALU.subtract)
            e.drain()
            e.tensor_scalar(SCR8B, ALPHA, -1.0, 1.0, op0=ALU.mult, op1=ALU.add)
            e.tensor_scalar(SCR8A, SCR8A, -1.0, 1.0, op0=ALU.mult, op1=ALU.add)
            e.drain()
            e.tensor_single_scalar(ZB, SCR8B, 0.0, op=ALU.is_gt)
            e.drain()
            e.tensor_scalar(CARC8 := CZ[:, 96:104], ZB, -1.0, 1.0, op0=ALU.mult, op1=ALU.add)
            e.drain()
            e.tensor_tensor(SCR8B, SCR8B, CZ[:, 96:104], op=ALU.add)
            e.drain()
            e.reciprocal(CZ[:, 96:104], SCR8B)
            e.drain()
            e.tensor_tensor(SCR8A, SCR8A, CZ[:, 96:104], op=ALU.mult)
            e.drain()
            e.tensor_tensor(SCR8A, SCR8A, ZB, op=ALU.mult)
            e.drain()
            e.tensor_scalar(SCR8B, ZB, -float(T), float(T), op0=ALU.mult, op1=ALU.add)
            e.drain()
            e.tensor_tensor(ZB, SCR8A, SCR8B, op=ALU.add)
            return e.drain()
        pl.add("vector", _cv2, waits=[(SS, SS.n)], inc=SV)
        t_consts_done = SS.n
        t_consts_done_v = SV.n

        xstg = [fr(0, 4096), fr(4096, 4096)]
        t_xtile = [0] * NT
        sX = [sems["sXA"], sems["sXB"]]
        gateX = [None, None]
        for g in range(4):
            gpar = g % 2
            w = [gateX[gpar]] if gateX[gpar] else []
            pl.add("gpsimd", lambda e, o=xstg[gpar], g=g:
                   e.dma_start(out=o.rearrange("p (t c) -> p t c", c=C),
                               in_=d_xs[g * 512:(g + 1) * 512, :]
                               .rearrange("(t p) c -> p t c", p=P)),
                   waits=w, inc=sX[gpar], inck=16)
            xtok = (sX[gpar], sX[gpar].n)
            for tl in range(4):
                tt = g * 4 + tl
                par = tt % 2
                tok = xtok
                bsel = (2 * par, 2 * par + 1)
                wv = [tok, (SIN, t_consts)]
                for b in bsel:
                    if bank_gate[b]:
                        wv.append(bank_gate[b])

                def _tp(e, gpar=gpar, tl=tl, bsel=bsel):
                    r = None
                    for k in range(CT):
                        b = bsel[k // 4]
                        r = e.transpose(PS[b][:, (k % 4) * 128:(k % 4) * 128 + 128],
                                        xstg[gpar][:, tl * 1024 + k * 128: tl * 1024 + k * 128 + 128],
                                        IDF[:])
                    return r
                pl.add("tensor", _tp, waits=wv, inc=ST)
                t_tp = ST.n

                def _ev(e, tt=tt, bsel=bsel):
                    r = None
                    for kk in range(2):
                        for k in range(4):
                            r = e.tensor_copy(xr(kk * 4 + k)[:, tt * 128:(tt + 1) * 128],
                                              PS[bsel[kk]][:, k * 128:k * 128 + 128])
                    return r
                pl.add("vector", _ev, waits=[(ST, t_tp)], inc=SV)
                t_xtile[tt] = SV.n
                for b in bsel:
                    bank_gate[b] = (SV, SV.n)
            gateX[gpar] = (SV, SV.n)
        for k in range(CT):
            t_xr[k] = SV.n

        wv = [(SWG, t_wg0)]
        if bank_gate[5]:
            wv.append(bank_gate[5])

        def _wrt(e):
            r = None
            for kt in range(CT):
                r = e.transpose(PS[5][:, kt * 4:kt * 4 + 4].bitcast(BF16),
                                RWB[:, kt * 128:(kt + 1) * 128], IDB[0:8, 0:8])
            return r
        pl.add("tensor", _wrt, waits=wv, inc=ST)
        pl.add("vector", lambda e: e.tensor_copy(WRT[:], PS[5][:, 0:32].bitcast(BF16)),
               waits=[(ST, ST.n)], inc=SV)
        bank_gate[5] = (SV, SV.n)
        set_ag()

        def plan_layernorm(sq_gate_in=None, xr_gate=None):
            sq2 = [fr(16400, 512), fr(16912, 512)]
            nmrow = A2[0:1, 17424:17936]
            rsrow = A2[0:1, 17936:18448]
            srow = A2[0:1, 12288:12800]
            sg = agw() if sq_gate_in is None else list(sq_gate_in)
            for b in range(4):
                cols = slice(b * 512, (b + 1) * 512)
                t_prev_mm = 0
                t_mm_hist = [0, 0]
                for k in range(CT):
                    kp = k % 2
                    xg = [(SV, t_xr[k])] if xr_gate is None else list(xr_gate(k, b))
                    pl.add("scalar", lambda e, k=k, cols=cols, kp=kp:
                           e.activation(sq2[kp][:, 0:512], xr(k)[:, cols], AF.Square),
                           waits=xg + [(ST, t_mm_hist[kp])] + sg, inc=SS)
                    wv = [(SS, SS.n)]
                    if k == 0:
                        for bb_ in (6, 7):
                            if bank_gate[bb_]:
                                wv.append(bank_gate[bb_])
                        wv.append((SV, t_ones))

                    def _mm(e, k=k, cols=cols, kp=kp):
                        e.matmul(PS[6][0:1], ONEC[:], xr(k)[:, cols],
                                 start=(k == 0), stop=(k == CT - 1))
                        return e.matmul(PS[7][0:1], ONEC[:], sq2[kp][:, 0:512],
                                        start=(k == 0), stop=(k == CT - 1))
                    pl.add("tensor", _mm, waits=wv, inc=ST)
                    t_mm_hist[kp] = ST.n
                    t_prev_mm = ST.n
                t_stats = ST.n

                def _rows_v(e):
                    e.tensor_scalar(nmrow, PS[6][0:1], -1.0 / C, None, op0=ALU.mult)
                    e.drain()
                    e.tensor_tensor(rsrow, nmrow, nmrow, op=ALU.mult)
                    e.drain()
                    e.scalar_tensor_tensor(rsrow, rsrow, -float(C), PS[7][0:1],
                                           op0=ALU.mult, op1=ALU.add)
                    return e.drain()
                pl.add("vector", _rows_v, waits=[(ST, t_stats)] + sg, inc=SV)
                bank_gate[6] = (SV, SV.n)
                bank_gate[7] = (SV, SV.n)
                def _rstd(e):
                    e.activation(srow, rsrow, AF.Ln, scale=1.0 / C, bias=EPSB[0:1, 0:1])
                    e.drain()
                    return e.activation(rsrow, srow, AF.Exp, scale=-0.5)
                pl.add("scalar", _rstd, waits=[(SV, SV.n)], inc=SS)
                t_rows = SS.n

                wv = [(SS, t_rows)]
                for bb_ in (4, 5):
                    if bank_gate[bb_]:
                        wv.append(bank_gate[bb_])

                def _bc(e):
                    e.matmul(PS[4], ONERF[:], nmrow, start=True, stop=True)
                    return e.matmul(PS[5], ONERF[:], rsrow, start=True, stop=True)
                pl.add("tensor", _bc, waits=wv, inc=ST)
                t_bc = ST.n

                for k in range(CT):
                    def _nrm(e, k=k, cols=cols):
                        tmpr = A2[:, 12288 + (k % 2) * 512: 12288 + (k % 2) * 512 + 512]
                        e.tensor_tensor(tmpr, xr(k)[:, cols], PS[4], op=ALU.add)
                        e.drain()
                        return e.tensor_tensor(xn(k)[:, cols], tmpr, PS[5], op=ALU.mult)
                    pl.add("vector", _nrm, waits=[(ST, t_bc)], inc=SV)
                    t_xn[k] = SV.n
                bank_gate[4] = (SV, SV.n)
                bank_gate[5] = (SV, SV.n)

        def plan_router(entry_gate=()):
            s_sb = AR1F[0:8, 0:2048]
            sel_sb = AR1F[0:8, 2048:4096]
            comb_sb = AR1F[0:8, 4096:6144]
            selT = AR1F[:, 6144:6272]
            sT_ = AR1F[:, 6272:6400]
            combT = AR1F[:, 6400:6528]
            m1 = AR1F[:, 6528:6544]
            m2 = AR1F[:, 6544:6560]
            s1 = AR1F[:, 6560:6576]
            s2 = AR1F[:, 6576:6592]
            den = AR1F[:, 6592:6608]
            rec = AR1F[:, 6608:6624]
            cond = AR1F[:, 6624:6640]
            g1 = AR1F[:, 6640:6656]
            g2 = AR1F[:, 6656:6672]
            denp = AR1F[:, 6672:6688]
            tmp3 = AR1F[:, 6784:6912]
            comb_bf = A2B[0:8, 32800:34848]

            for nb in range(4):
                for kt in range(CT):
                    wv = [(SWG, t_wg0), (SV, t_xn[kt])] + agw()
                    if kt == 0 and bank_gate[nb]:
                        wv.append(bank_gate[nb])
                    pl.add("tensor", lambda e, nb=nb, kt=kt: e.matmul(
                        PS[nb][0:8], WRT[:, kt * 8:(kt + 1) * 8],
                        xn(kt)[:, nb * 512:(nb + 1) * 512],
                        start=(kt == 0), stop=(kt == CT - 1)), waits=wv, inc=ST)
            t_lg = ST.n

            def _sg(e):
                r = None
                for nb in range(4):
                    r = e.activation(s_sb[:, nb * 512:(nb + 1) * 512], PS[nb][0:8], AF.Sigmoid)
                return r
            pl.add("scalar", _sg, waits=[(ST, t_lg)] + agw(), inc=SS)
            for nb in range(4):
                bank_gate[nb] = (SS, SS.n)
            pl.add("vector", lambda e: e.tensor_scalar(sel_sb, s_sb, RBIA[:, 0:1], None, op0=ALU.add),
                   waits=[(SS, SS.n), (SIN, t_consts)] + agw() + list(entry_gate), inc=SV)
            t_sel = SV.n

            wv = [(SV, t_sel)]
            if bank_gate[4]:
                wv.append(bank_gate[4])

            def _tps(e):
                r = None
                for tt in range(NT):
                    r = e.transpose(PS[4][:, tt * 8:(tt + 1) * 8],
                                    sel_sb[:, tt * 128:(tt + 1) * 128], IDF[0:8, 0:8])
                return r
            pl.add("tensor", _tps, waits=wv, inc=ST)
            wv = [(SV, t_sel)]
            if bank_gate[5]:
                wv.append(bank_gate[5])

            def _tps2(e):
                r = None
                for tt in range(NT):
                    r = e.transpose(PS[5][:, tt * 8:(tt + 1) * 8],
                                    s_sb[:, tt * 128:(tt + 1) * 128], IDF[0:8, 0:8])
                return r
            pl.add("tensor", _tps2, waits=wv, inc=ST)
            t_tp = ST.n

            def _ev(e):
                e.tensor_copy(selT, PS[4][:, 0:128])
                e.tensor_copy(sT_, PS[5][:, 0:128])
                return e.drain()
            pl.add("vector", _ev, waits=[(ST, t_tp)], inc=SV)
            bank_gate[4] = (SV, SV.n)
            bank_gate[5] = (SV, SV.n)

            sel3 = selT.rearrange("p (g e) -> p g e", e=8)
            s3 = sT_.rearrange("p (g e) -> p g e", e=8)
            comb3 = combT.rearrange("p (g e) -> p g e", e=8)
            tmp33 = tmp3.rearrange("p (g e) -> p g e", e=8)
            m1b = m1.rearrange("p (g o) -> p g o", o=1).broadcast_to((P, 16, 8))
            m2b = m2.rearrange("p (g o) -> p g o", o=1).broadcast_to((P, 16, 8))
            g1b = g1.rearrange("p (g o) -> p g o", o=1).broadcast_to((P, 16, 8))
            g2b = g2.rearrange("p (g o) -> p g o", o=1).broadcast_to((P, 16, 8))

            def _top2(e):
                def d(x):
                    e.drain()
                    return x
                d(e.tensor_reduce(m1, sel3, axis=AX.X, op=ALU.max))
                d(e.tensor_tensor(tmp33, sel3, m1b, op=ALU.is_ge))
                d(e.tensor_tensor(combT, tmp3, sT_, op=ALU.mult))
                d(e.tensor_reduce(s1, combT.rearrange("p (g e) -> p g e", e=8), axis=AX.X, op=ALU.add))
                d(e.scalar_tensor_tensor(combT, tmp3, -1e9, selT, op0=ALU.mult, op1=ALU.add))
                d(e.tensor_reduce(m2, comb3, axis=AX.X, op=ALU.max))
                d(e.tensor_tensor(comb3, comb3, m2b, op=ALU.is_ge))
                d(e.tensor_tensor(sT_, combT, sT_, op=ALU.mult))
                d(e.tensor_reduce(s2, s3, axis=AX.X, op=ALU.add))
                d(e.tensor_tensor(den, s1, s2, op=ALU.add))
                d(e.tensor_scalar(denp, den, 1e-9, None, op0=ALU.add))
                d(e.reciprocal(rec, denp))
                d(e.tensor_single_scalar(cond, den, 1e-9, op=ALU.is_gt))
                d(e.tensor_tensor(g1, s1, rec, op=ALU.mult))
                d(e.tensor_scalar(g1, g1, -0.5, None, op0=ALU.add))
                d(e.tensor_tensor(g1, g1, cond, op=ALU.mult))
                d(e.tensor_scalar(g1, g1, 0.5, 0.25, op0=ALU.mult, op1=ALU.add))
                d(e.tensor_tensor(g2, s2, rec, op=ALU.mult))
                d(e.tensor_scalar(g2, g2, -0.5, None, op0=ALU.add))
                d(e.tensor_tensor(g2, g2, cond, op=ALU.mult))
                d(e.tensor_scalar(g2, g2, 0.5, 0.25, op0=ALU.mult, op1=ALU.add))
                d(e.tensor_tensor(tmp33, tmp33, g1b, op=ALU.mult))
                d(e.tensor_tensor(comb3, comb3, g2b, op=ALU.mult))
                return e.tensor_tensor(combT, combT, tmp3, op=ALU.add)
            pl.add("vector", _top2, waits=[(SV, SV.n)], inc=SV)
            t_top2 = SV.n

            for grp in range(4):
                wv = [(SV, t_top2)]
                if bank_gate[5]:
                    wv.append(bank_gate[5])

                def _tb(e, grp=grp):
                    r = None
                    for t4 in range(4):
                        tt = grp * 4 + t4
                        r = e.transpose(PS[5][0:8, t4 * 128:(t4 + 1) * 128],
                                        combT[:, tt * 8:(tt + 1) * 8], IDF[:])
                    return r
                pl.add("tensor", _tb, waits=wv, inc=ST)
                pl.add("vector", lambda e, grp=grp: e.tensor_copy(
                    comb_sb[:, grp * 512:(grp + 1) * 512], PS[5][0:8]),
                    waits=[(ST, ST.n)], inc=SV)
                bank_gate[5] = (SV, SV.n)
            pl.add("vector", lambda e: (e.drain(), e.tensor_copy(comb_bf, comb_sb))[1],
                   waits=[(SV, SV.n)], inc=SV)
            return comb_bf, SV.n

        sw2_gate = [None]
        sq_gate = [(ST, 0)]
        ln_sq_gate = [None]
        ln_xr_gate = [None]
        ffn_extra = [None]

        def plan_shared(t_nrm):
            s1b = [br(0, 1024), br(1024, 1024)]
            s3b = [br(2048, 1024), br(3072, 1024)]
            sab = [br(4096, 1024), br(5120, 1024)]
            sw2b = br(16384, 16384)
            sW1 = [sems["sW1A"], sems["sW1B"]]
            sW3 = [sems["sW3A"], sems["sW3B"]]
            gW1 = [None, None]
            gW3 = [None, None]
            w = [sw2_gate[0]] if sw2_gate[0] else []
            pl.add("gpsimd", lambda e: e.dma_start(
                out=sw2b.rearrange("p (kt j) -> p kt j", j=C),
                in_=d_sw2.rearrange("(kt p) j -> p kt j", p=P)),
                waits=w + agw() + [(SV, t_nrm)], inc=sems["sSW2"], inck=16)
            t_sw2 = (sems["sSW2"], sems["sSW2"].n)

            xr_ticks = [0] * CT
            t_ab_last = [0]
            t_g_last = [0]
            t_silu_last = [0]
            for nb in range(2):
                g_ticks = [0] * 16
                for m in range(16):
                    par = m % 2
                    toks = []
                    for (dmat, buf, sS_, gX_) in ((d_sw1, s1b, sW1, gW1),
                                                  (d_sw3, s3b, sW3, gW3)):
                        w = [gX_[par]] if gX_[par] else [sq_gate[0]]
                        pl.add("gpsimd", lambda e, o=buf[par], m=m, dm=dmat:
                               e.dma_start(out=o.rearrange("p (kt j) -> p kt j", j=128),
                                           in_=dm[:, m * 128:(m + 1) * 128]
                                           .rearrange("(kt p) j -> p kt j", p=P)),
                               waits=w + agw(), inc=sS_[par], inck=16)
                        toks.append((sS_[par], sS_[par].n))
                    bsa = (2 * par, 2 * par + 1)
                    bsb = (4 + 2 * par, 5 + 2 * par)
                    for kt in range(CT):
                        wv = list(toks) + [(SV, t_xn[kt])]
                        if kt == 0:
                            for b in bsa + bsb:
                                if bank_gate[b]:
                                    wv.append(bank_gate[b])

                        def _mmk(e, par=par, nb=nb, bsa=bsa, bsb=bsb, kt=kt):
                            r = None
                            for (buf, bs) in ((s1b, bsa), (s3b, bsb)):
                                for nn in range(2):
                                    r = e.matmul(PS[bs[nn]],
                                                 buf[par][:, kt * 128:(kt + 1) * 128],
                                                 xn(kt)[:, nb * 1024 + nn * 512: nb * 1024 + (nn + 1) * 512],
                                                 start=(kt == 0), stop=(kt == CT - 1))
                            return r
                        pl.add("tensor", _mmk, waits=wv, inc=ST)
                    t_mm = ST.n
                    t_ab_last[0] = max(t_ab_last[0], t_mm)

                    def _silu(e, par=par, bsa=bsa):
                        e.activation(sab[par][:, 0:512], PS[bsa[0]], AF.Silu)
                        return e.activation(sab[par][:, 512:1024], PS[bsa[1]], AF.Silu)
                    pl.add("scalar", _silu, waits=[(ST, t_mm)], inc=SS)
                    t_silu = SS.n
                    t_silu_last[0] = max(t_silu_last[0], t_silu)
                    for b in bsa:
                        bank_gate[b] = (SS, t_silu)
                    garena = AR1[:, m * 1024:(m + 1) * 1024]

                    def _g(e, par=par, m=m, bsb=bsb, garena=garena):
                        e.scalar_tensor_tensor(garena[:, 0:512], sab[par][:, 0:512], 0.5,
                                               PS[bsb[0]], op0=ALU.mult, op1=ALU.mult)
                        return e.scalar_tensor_tensor(garena[:, 512:1024], sab[par][:, 512:1024], 0.5,
                                                      PS[bsb[1]], op0=ALU.mult, op1=ALU.mult)
                    pl.add("vector", _g, waits=[(SS, t_silu), (ST, t_mm)], inc=SV)
                    g_ticks[m] = SV.n
                    t_g_last[0] = max(t_g_last[0], SV.n)
                    for b in bsb:
                        bank_gate[b] = (SV, SV.n)
                    gW1[par] = (SV, SV.n)
                    gW3[par] = (SV, SV.n)
                for dm in range(CT):
                    par = dm % 2
                    bso = (2 * par, 2 * par + 1)
                    if dm == 0:
                        for kt in range(16):
                            wv = [t_sw2, (SV, g_ticks[kt])]
                            if kt == 0:
                                for b in bso:
                                    if bank_gate[b]:
                                        wv.append(bank_gate[b])

                            def _mok(e, dm=dm, bso=bso, kt=kt):
                                r = None
                                for nn in range(2):
                                    r = e.matmul(PS[bso[nn]],
                                                 sw2b[:, kt * 1024 + dm * 128: kt * 1024 + (dm + 1) * 128],
                                                 AR1[:, kt * 1024 + nn * 512: kt * 1024 + (nn + 1) * 512],
                                                 start=(kt == 0), stop=(kt == 15))
                                return r
                            pl.add("tensor", _mok, waits=wv, inc=ST)
                        t_mo = ST.n
                    else:
                        wv = [t_sw2, (SV, max(g_ticks))]
                        for b in bso:
                            if bank_gate[b]:
                                wv.append(bank_gate[b])

                        def _mo(e, dm=dm, bso=bso):
                            r = None
                            for nn in range(2):
                                for kt in range(16):
                                    r = e.matmul(PS[bso[nn]],
                                                 sw2b[:, kt * 1024 + dm * 128: kt * 1024 + (dm + 1) * 128],
                                                 AR1[:, kt * 1024 + nn * 512: kt * 1024 + (nn + 1) * 512],
                                                 start=(kt == 0), stop=(kt == 15))
                            return r
                        pl.add("tensor", _mo, waits=wv, inc=ST)
                        t_mo = ST.n

                    def _acc(e, dm=dm, nb=nb, bso=bso):
                        e.tensor_tensor(xr(dm)[:, nb * 1024:nb * 1024 + 512],
                                        xr(dm)[:, nb * 1024:nb * 1024 + 512], PS[bso[0]][:], op=ALU.add)
                        return e.tensor_tensor(xr(dm)[:, nb * 1024 + 512:(nb + 1) * 1024],
                                               xr(dm)[:, nb * 1024 + 512:(nb + 1) * 1024],
                                               PS[bso[1]][:], op=ALU.add)
                    pl.add("vector", _acc, waits=[(ST, t_mo), (SV, t_xr[dm])], inc=SV)
                    xr_ticks[dm] = max(xr_ticks[dm], SV.n)
                    for b in bso:
                        bank_gate[b] = (SV, SV.n)
            sw2_gate[0] = (ST, ST.n)
            ab_gate = [(ST, t_ab_last[0]), (SV, t_g_last[0]), (SS, t_silu_last[0])]
            return xr_ticks, ab_gate

        e_gates = {"e1": [None, None], "e2": [None]}

        def plan_experts(comb_bf, t_comb, gate_e1, gate_e2, dbg_stop=None):
            exp_ticks = {"hp": 0, "cb_mm": 0}
            e_gates["e1"] = [None, None]
            e_gates["e2"] = [None]
            e1b = [br(0, 8192), br(8192, 8192)]
            e2b = br(16384, 8192)
            cbb = [br(24576, 2048), br(26624, 2048)]
            hbuf = AR1[:, 0:8192]
            hpbuf = AR1[:, 8192:16384]
            sE1 = [sems["sE1A"], sems["sE1B"]]
            sE2 = sems["sE2"]
            xr_ticks = [0] * CT

            for ee in range(E):
                par = ee % 2
                w = [e_gates["e1"][par]] if e_gates["e1"][par] else list(gate_e1)
                pl.add("gpsimd", lambda e, o=e1b[par], ee=ee: e.dma_start(
                    out=o.rearrange("p (kt j) -> p kt j", j=C),
                    in_=d_ew1[ee].rearrange("(kt p) j -> p kt j", p=P)),
                    waits=w + agw(), inc=sE1[par], inck=16)
                t_e1 = (sE1[par], sE1[par].n)
                w = [e_gates["e2"][0]] if e_gates["e2"][0] else list(gate_e2)
                pl.add("gpsimd", lambda e, ee=ee: e.dma_start(
                    out=e2b.rearrange("p (kt j) -> p kt j", j=C),
                    in_=d_ew2[ee].rearrange("(kt p) j -> p kt j", p=P)),
                    waits=w + agw(), inc=sE2, inck=16)
                t_e2 = (sE2, sE2.n)

                cpar = ee % 2
                cbank = 6 + cpar
                wv = [(SV, t_comb)]
                for nb4 in range(4):
                    wv2 = list(wv)
                    if bank_gate[cbank]:
                        wv2.append(bank_gate[cbank])

                    def _cb1(e, ee=ee, cbank=cbank, nb4=nb4):
                        return e.matmul(PS[cbank], OH8[:, ee * P:(ee + 1) * P],
                                        comb_bf[0:8, nb4 * 512:(nb4 + 1) * 512],
                                        start=True, stop=True)
                    pl.add("tensor", _cb1, waits=wv2 + [(SWG, t_wg0)], inc=ST)
                    exp_ticks["cb_mm"] = ST.n
                    pl.add("vector", lambda e, cpar=cpar, nb4=nb4, cbank=cbank: e.tensor_copy(
                        cbb[cpar][:, nb4 * 512:(nb4 + 1) * 512], PS[cbank]),
                        waits=[(ST, ST.n)], inc=SV)
                    bank_gate[cbank] = (SV, SV.n)
                t_cb = SV.n

                for nb in range(2):
                    hp_ticks = [0] * CT
                    for hm in range(CT):
                        hpar = hm % 2
                        bsh = (2 * hpar, 2 * hpar + 1)
                        wv = [t_e1, (SV, max(t_xn))]
                        for b in bsh:
                            if bank_gate[b]:
                                wv.append(bank_gate[b])

                        def _hmm(e, hm=hm, nb=nb, par=par, bsh=bsh):
                            r = None
                            for nn in range(2):
                                for kt in range(CT):
                                    r = e.matmul(PS[bsh[nn]],
                                                 e1b[par][:, kt * 1024 + hm * 128: kt * 1024 + (hm + 1) * 128],
                                                 xn(kt)[:, nb * 1024 + nn * 512: nb * 1024 + (nn + 1) * 512],
                                                 start=(kt == 0), stop=(kt == CT - 1))
                            return r
                        pl.add("tensor", _hmm, waits=wv, inc=ST)
                        t_h = ST.n

                        def _gel(e, hm=hm, bsh=bsh):
                            e.activation(hbuf[:, hm * 1024:hm * 1024 + 512], PS[bsh[0]][:], AF.Gelu)
                            return e.activation(hbuf[:, hm * 1024 + 512:(hm + 1) * 1024],
                                                PS[bsh[1]][:], AF.Gelu)
                        gw = [(ST, t_h)]
                        if ee == 0 and nb == 0:
                            gw.append((SV, t_comb))
                        pl.add("scalar", _gel, waits=gw, inc=SS)
                        t_g = SS.n
                        for b in bsh:
                            bank_gate[b] = (SS, t_g)

                        def _hp(e, hm=hm, nb=nb, cpar=cpar):
                            return e.tensor_tensor(hpbuf[:, hm * 1024:(hm + 1) * 1024],
                                                   hbuf[:, hm * 1024:(hm + 1) * 1024],
                                                   cbb[cpar][:, nb * 1024:(nb + 1) * 1024], op=ALU.mult)
                        pl.add("vector", _hp, waits=[(SS, t_g), (SV, t_cb)], inc=SV)
                        hp_ticks[hm] = SV.n
                        exp_ticks["hp"] = SV.n
                    if dbg_stop == "h0":
                        pl.add("vector", lambda e: e.tensor_copy(A2[:, 0:16384], AR1[:]),
                               waits=[(SV, max(hp_ticks))], inc=SV)
                        return ("h0", SV.n)
                    for dm in range(CT):
                        ypar = dm % 2
                        bsy = (4 + 2 * ypar, 5 + 2 * ypar)
                        if dm == 0:
                            for kt in range(CT):
                                wv = [t_e2, (SV, hp_ticks[kt])]
                                if kt == 0:
                                    for b in bsy:
                                        if bank_gate[b]:
                                            wv.append(bank_gate[b])

                                def _ymmk(e, dm=dm, bsy=bsy, kt=kt):
                                    r = None
                                    for nn in range(2):
                                        r = e.matmul(PS[bsy[nn]],
                                                     e2b[:, kt * 1024 + dm * 128: kt * 1024 + (dm + 1) * 128],
                                                     hpbuf[:, kt * 1024 + nn * 512: kt * 1024 + (nn + 1) * 512],
                                                     start=(kt == 0), stop=(kt == CT - 1))
                                    return r
                                pl.add("tensor", _ymmk, waits=wv, inc=ST)
                            t_y = ST.n
                        else:
                            wv = [t_e2, (SV, max(hp_ticks))]
                            for b in bsy:
                                if bank_gate[b]:
                                    wv.append(bank_gate[b])

                            def _ymm(e, dm=dm, bsy=bsy):
                                r = None
                                for nn in range(2):
                                    for kt in range(CT):
                                        r = e.matmul(PS[bsy[nn]],
                                                     e2b[:, kt * 1024 + dm * 128: kt * 1024 + (dm + 1) * 128],
                                                     hpbuf[:, kt * 1024 + nn * 512: kt * 1024 + (nn + 1) * 512],
                                                     start=(kt == 0), stop=(kt == CT - 1))
                                return r
                            pl.add("tensor", _ymm, waits=wv, inc=ST)
                            t_y = ST.n

                        def _acc(e, dm=dm, nb=nb, bsy=bsy):
                            e.tensor_tensor(xr(dm)[:, nb * 1024:nb * 1024 + 512],
                                            xr(dm)[:, nb * 1024:nb * 1024 + 512], PS[bsy[0]][:], op=ALU.add)
                            return e.tensor_tensor(xr(dm)[:, nb * 1024 + 512:(nb + 1) * 1024],
                                                   xr(dm)[:, nb * 1024 + 512:(nb + 1) * 1024],
                                                   PS[bsy[1]][:], op=ALU.add)
                        pl.add("vector", _acc, waits=[(ST, t_y), (SV, t_xr[dm])], inc=SV)
                        xr_ticks[dm] = max(xr_ticks[dm], SV.n)
                        for b in bsy:
                            bank_gate[b] = (SV, SV.n)
                e_gates["e1"][par] = (ST, ST.n)
                e_gates["e2"][0] = (ST, ST.n)
            return xr_ticks, exp_ticks

        def plan_ffn(t_ready):
            for k in range(CT):
                t_xr[k] = max(t_xr[k], t_ready[k])
            plan_layernorm(sq_gate_in=ln_sq_gate[0], xr_gate=ln_xr_gate[0])
            ln_sq_gate[0] = None
            ln_xr_gate[0] = None
            sq_gate[0] = (ST, ST.n)
            tick1, ab_gate = plan_shared(max(t_xn))
            t_sw2_end = ST.n
            comb_bf, t_comb = plan_router(entry_gate=[(ST, t_sw2_end)])
            gate_e2 = [(ST, ST.n), (SV, SV.n), (SS, SS.n)]
            tick2, exp_ticks = plan_experts(comb_bf, t_comb, ab_gate, gate_e2)
            ffn_extra[0] = {"sw2_end": t_sw2_end, "hp": exp_ticks["hp"],
                            "cb_mm": exp_ticks["cb_mm"], "comb_v": t_comb}
            return [max(a, b) for a, b in zip(tick1, tick2)]

        def plan_attn(t_ready):
            for k in range(CT):
                t_xr[k] = max(t_xr[k], t_ready[k])
            fx = ffn_extra[0]
            plan_layernorm(sq_gate_in=[(ST, fx["cb_mm"]), (SV, fx["comb_v"])])

            wvalT = br(0, 8192)
            wrecT = br(8192, 8192)
            woutT = br(16384, 8192)
            wstg = br(24576, 8192)
            sWS = sems["sWS"]
            wT_ticks = {}
            gstg = None

            def prep_weight(dmat, dstT, extra_dma_waits=(), evac_waits=()):
                nonlocal gstg
                pl.add("gpsimd", lambda e, dm=dmat: e.dma_start(
                    out=wstg.rearrange("p (dt j) -> p dt j", j=C),
                    in_=dm.rearrange("(dt p) j -> p dt j", p=P)),
                    waits=list(gstg if gstg is not None else agw()) + list(extra_dma_waits),
                    inc=sWS, inck=16)
                t_l = (sWS, sWS.n)
                for kt in range(CT):
                    par = kt % 2
                    bank = 6 + par
                    wv = [t_l, (SWG, t_wg0)]
                    if bank_gate[bank]:
                        wv.append(bank_gate[bank])

                    def _tpw(e, kt=kt, bank=bank):
                        r = None
                        for dt in range(CT):
                            r = e.transpose(PS[bank][:, dt * 64:dt * 64 + 64].bitcast(BF16),
                                            wstg[:, dt * 1024 + kt * 128: dt * 1024 + (kt + 1) * 128],
                                            IDB[:])
                        return r
                    pl.add("tensor", _tpw, waits=wv, inc=ST)
                    pl.add("vector", lambda e, kt=kt, bank=bank, dstT=dstT: e.tensor_copy(
                        dstT[:, kt * 1024:(kt + 1) * 1024], PS[bank][:, 0:512].bitcast(BF16)),
                        waits=[(ST, ST.n)] + list(evac_waits), inc=SV)
                    bank_gate[bank] = (SV, SV.n)
                gstg = [(ST, ST.n)]
                return SV.n

            sq_gate[0] = (ST, ST.n)
            t_nrm_done = SV.n
            early = [(ST, fx["sw2_end"]), (SV, max(fx["hp"], t_nrm_done))]
            gstg = early
            wT_ticks["val"] = prep_weight(d_val, wvalT, evac_waits=[sq_gate[0]])
            wT_ticks["rec"] = prep_weight(d_rec, wrecT)

            zA = fr(12288, 2048)
            zO = fr(14344, 2048)

            t_sigr = [0] * CT
            for ct in range(CT):
                for kt in range(CT):
                    wv = [(SV, max(wT_ticks["val"], t_xn[kt]))]
                    if kt == 0:
                        for b in range(4):
                            if bank_gate[b]:
                                wv.append(bank_gate[b])

                    def _vmmk(e, ct=ct, kt=kt):
                        r = None
                        for nb in range(4):
                            r = e.matmul(PS[nb],
                                         wvalT[:, kt * 1024 + ct * 128: kt * 1024 + (ct + 1) * 128],
                                         xn(kt)[:, nb * 512:(nb + 1) * 512],
                                         start=(kt == 0), stop=(kt == CT - 1))
                        return r
                    pl.add("tensor", _vmmk, waits=wv, inc=ST)
                t_v = ST.n
                wv = [(SV, wT_ticks["rec"])]
                for b in range(4, 8):
                    if bank_gate[b]:
                        wv.append(bank_gate[b])

                def _rmm(e, ct=ct):
                    r = None
                    for nb in range(4):
                        for kt in range(CT):
                            r = e.matmul(PS[4 + nb],
                                         wrecT[:, kt * 1024 + ct * 128: kt * 1024 + (ct + 1) * 128],
                                         xn(kt)[:, nb * 512:(nb + 1) * 512],
                                         start=(kt == 0), stop=(kt == CT - 1))
                    return r
                pl.add("tensor", _rmm, waits=wv, inc=ST)
                t_r = ST.n

                albcT = ALPHA[:, ct:ct + 1].broadcast_to((P, T))
                pl.add("vector", lambda e, albcT=albcT: (e.tensor_tensor_scan(
                    zO[:, 0:T], albcT, PT0[:], 0.0, op0=ALU.mult, op1=ALU.add), e.drain())[1],
                    waits=[(ST, t_v), (SS, t_consts_done), (SV, t_ones)], inc=SV)
                for b in range(4):
                    bank_gate[b] = (SV, SV.n)
                pl.add("vector", lambda e, ct=ct: e.tensor_copy(ZS8[:, ct:ct + 1], zO[:, T - 1:T]),
                       waits=[(SV, SV.n)], inc=SV)
                t_zc = SV.n
                pl.add("gpsimd", lambda e, ct=ct: e.dma_start(
                    out=cc_in.ap()[ct:ct + 1, :].rearrange("o p -> p o"), in_=ZS8[:, ct:ct + 1]),
                    waits=[(SV, t_zc)], inc=SWG, inck=16)


                def _sig(e, ct=ct):
                    r = None
                    for nb in range(4):
                        r = e.activation(ar1(ct)[:, nb * 512:(nb + 1) * 512], PS[4 + nb], AF.Sigmoid)
                    return r
                pl.add("scalar", _sig, waits=[(ST, t_r)] + agw(), inc=SS)
                t_sigr[ct] = SS.n
                for b in range(4, 8):
                    bank_gate[b] = (SS, SS.n)
            t_zdma = SWG.n
            t_phA_v = SV.n

            pl.add("gpsimd", lambda e: e.collective_compute(
                "AllGather", ALU.bypass, replica_groups=[list(range(NCORES))],
                ins=[cc_in.ap()], outs=[cc_out.ap()]),
                waits=[(SWG, t_zdma)], inc=SG)
            t_cc = SG.n
            wT_ticks["out"] = prep_weight(d_out, woutT, extra_dma_waits=[(SV, t_phA_v)])
            t_wouttp = ST.n
            for ct in range(CT):
                pl.add("gpsimd", lambda e, ct=ct: e.dma_start(
                    out=ZG[:, ct * 8:(ct + 1) * 8],
                    in_=cc_out.ap().rearrange("(r c) p -> c p r", c=CT)[ct],
                    allow_slow_non_contiguous=True),
                    waits=[(SG, t_cc)], inc=SWG, inck=16)
            t_zg = SWG.n

            AARG = fr(16400, 64)
            APRD = fr(16464, 64)
            ex3 = EXPO2.rearrange("p (o e) -> p o e", o=1).broadcast_to((P, 8, 8))
            mk3 = MASK2.rearrange("p (o e) -> p o e", o=1).broadcast_to((P, 8, 8))
            nw3 = N2RW.rearrange("p (c o) -> p c o", o=1).broadcast_to((P, 8, 8))
            pl.add("vector", lambda e: (e.tensor_tensor(
                AARG.rearrange("p (c e) -> p c e", e=8), ex3, nw3, op=ALU.mult), e.drain())[1],
                waits=[(SV, t_phA_v), (SS, t_consts_done)], inc=SV)
            pl.add("scalar", lambda e: e.activation(AARG, AARG, AF.Exp),
                   waits=[(SV, SV.n)], inc=SS)

            def _combb(e):
                e.tensor_tensor(AARG.rearrange("p (c e) -> p c e", e=8),
                                AARG.rearrange("p (c e) -> p c e", e=8), mk3, op=ALU.mult)
                e.drain()
                e.tensor_tensor(APRD, AARG, ZG[:, 0:64], op=ALU.mult)
                e.drain()
                e.tensor_reduce(CDAT, APRD.rearrange("p (c e) -> p c e", e=8), axis=AX.X, op=ALU.add)
                e.drain()
                e.tensor_reduce(SCR8B, AARG.rearrange("p (c e) -> p c e", e=8), axis=AX.X, op=ALU.add)
                e.drain()
                e.tensor_tensor(CBB, SCR8B, ZB, op=ALU.mult)
                return e.drain()
            pl.add("vector", _combb, waits=[(SS, SS.n), (SWG, t_zg)], inc=SV)
            t_carry = SV.n

            zzf2 = [fr(12288, 2049), fr(16400, 2049)]
            zbf = fr(14344, 2049)
            numf2 = fr(6144, 2048)
            DEN = fr(4096, 2048)
            rw_ticks = [0] * CT
            t_dv = [0] * (CT + 1)
            t_scans = [0] * (CT + 1)
            t_den = [0] * (CT + 1)
            t_wkvA = [0] * (CT + 1)
            t_wkvB = [0] * (CT + 1)
            t_vmm = [0] * (CT + 1)

            def plan_vmm(ct):
                wv = [(SV, wT_ticks["val"])]
                if ct > 0:
                    wv += [(SV, t_scans[ct - 1])]
                else:
                    for b in range(4):
                        if bank_gate[b]:
                            wv.append(bank_gate[b])

                def _vmm2(e, ct=ct):
                    r = None
                    for nb in range(4):
                        for kt in range(CT):
                            r = e.matmul(PS[nb],
                                         wvalT[:, kt * 1024 + ct * 128: kt * 1024 + (ct + 1) * 128],
                                         xn(kt)[:, nb * 512:(nb + 1) * 512],
                                         start=(kt == 0), stop=(kt == CT - 1))
                    return r
                pl.add("tensor", _vmm2, waits=wv, inc=ST)
                t_vmm[ct] = ST.n

            def plan_scans(ct):
                p = ct % 2
                albc = ALPHA[:, ct:ct + 1].broadcast_to((P, T))
                onbc = ONEC[:, 0:1].broadcast_to((P, T))

                def _scans(e, ct=ct, p=p, albc=albc, onbc=onbc):
                    e.tensor_copy(zzf2[p][:, 0:1], CDAT[:, ct:ct + 1])
                    e.tensor_copy(zbf[:, 0:1], CBB[:, ct:ct + 1])
                    e.tensor_tensor_scan(zzf2[p][:, 1:T + 1], albc, PT0[:], CDAT[:, ct:ct + 1],
                                         op0=ALU.mult, op1=ALU.add)
                    e.tensor_tensor_scan(zbf[:, 1:T + 1], albc, onbc, CBB[:, ct:ct + 1],
                                         op0=ALU.mult, op1=ALU.add)
                    return e.drain()
                wv = [(ST, max(t_vmm[ct], t_wouttp)), (SV, t_carry)]
                if ct > 0:
                    wv.append((SS, t_den[ct - 1]))
                pl.add("vector", _scans, waits=wv, inc=SV)
                t_scans[ct] = SV.n

            def plan_dv(ct):
                t_dv[ct] = SS.n

            def plan_den(ct):
                def _den(e, ct=ct):
                    e.activation(DEN, zbf[:, 0:T], AF.Relu, scale=GB[:, ct:ct + 1],
                                 bias=DELTAEPS[:, ct:ct + 1])
                    e.drain()
                    e.activation(zbf[:, 0:T], DEN, AF.Ln)
                    e.drain()
                    return e.activation(DEN, zbf[:, 0:T], AF.Exp, scale=-1.0)
                wv = [(SV, t_scans[ct]), (ST, t_wouttp)]
                if ct > 0:
                    wv.append((SV, t_wkvB[ct - 1]))
                pl.add("scalar", _den, waits=wv, inc=SS)
                t_den[ct] = SS.n

            def plan_wkvA(ct):
                p = ct % 2

                def _wkvA(e, ct=ct, p=p):
                    e.tensor_scalar(numf2, zzf2[p][:, 1:T + 1], DELTA[:, ct:ct + 1], None,
                                    op0=ALU.mult)
                    e.drain()
                    e.scalar_tensor_tensor(zzf2[p][:, 0:T], zzf2[p][:, 0:T], GBDA[:, ct:ct + 1],
                                           numf2, op0=ALU.mult, op1=ALU.add)
                    return e.drain()
                pl.add("vector", _wkvA, waits=[(SV, t_consts_done_v)], inc=SV)
                t_wkvA[ct] = SV.n

            def plan_wkvB(ct):
                p = ct % 2

                def _wkvB(e, ct=ct, p=p):
                    e.tensor_tensor(zzf2[p][:, 0:T], zzf2[p][:, 0:T], DEN, op=ALU.mult)
                    e.drain()
                    e.tensor_tensor(ar1(ct)[:, 0:T], zzf2[p][:, 0:T], ar1(ct)[:, 0:T], op=ALU.mult)
                    return e.drain()
                pl.add("vector", _wkvB,
                       waits=[(SS, max(t_den[ct], t_sigr[ct])), (SV, t_consts_done_v)], inc=SV)
                t_wkvB[ct] = SV.n
                rw_ticks[ct] = SV.n

            plan_vmm(0)
            plan_scans(0)
            plan_dv(0)
            plan_den(0)
            for ct in range(1, CT):
                plan_vmm(ct)
                plan_wkvA(ct - 1)
                plan_scans(ct)
                plan_dv(ct)
                plan_wkvB(ct - 1)
                plan_den(ct)
            plan_wkvA(CT - 1)
            plan_wkvB(CT - 1)

            xr_ticks = [0] * CT
            first_grp = [True]
            for dm in range(CT):
                for nb in range(4):
                    par = (dm * 4 + nb) % 2
                    bank = 4 + par
                    if first_grp[0]:
                        first_grp[0] = False
                        for kt in range(CT):
                            wv = [(SV, max(rw_ticks[kt], wT_ticks["out"]))]
                            if kt == 0:
                                for b in range(4, 8):
                                    if bank_gate[b]:
                                        wv.append(bank_gate[b])
                            pl.add("tensor", lambda e, dm=dm, nb=nb, bank=bank, kt=kt: e.matmul(
                                PS[bank],
                                woutT[:, kt * 1024 + dm * 128: kt * 1024 + (dm + 1) * 128],
                                ar1(kt)[:, nb * 512:(nb + 1) * 512],
                                start=(kt == 0), stop=(kt == CT - 1)), waits=wv, inc=ST)
                        t_o = ST.n
                    else:
                        wv = [(SV, max(max(rw_ticks), wT_ticks["out"]))]
                        if bank_gate[bank]:
                            wv.append(bank_gate[bank])

                        def _omm(e, dm=dm, nb=nb, bank=bank):
                            r = None
                            for kt in range(CT):
                                r = e.matmul(PS[bank],
                                             woutT[:, kt * 1024 + dm * 128: kt * 1024 + (dm + 1) * 128],
                                             ar1(kt)[:, nb * 512:(nb + 1) * 512],
                                             start=(kt == 0), stop=(kt == CT - 1))
                            return r
                        pl.add("tensor", _omm, waits=wv, inc=ST)
                        t_o = ST.n
                    pl.add("vector", lambda e, dm=dm, nb=nb, bank=bank: e.tensor_tensor(
                        xr(dm)[:, nb * 512:(nb + 1) * 512],
                        xr(dm)[:, nb * 512:(nb + 1) * 512], PS[bank], op=ALU.add),
                        waits=[(ST, t_o), (SV, t_xr[dm])], inc=SV)
                    xr_ticks[dm] = SV.n
                    bank_gate[bank] = (SV, SV.n)
            return xr_ticks

        def plan_out(t_ready):
            otok = [fr(0, 1024), fr(1024, 1024)]
            n_st = 0
            gateO = [None, None]
            for tt in range(NT):
                par = tt % 2
                bsel = (2 * par, 2 * par + 1)
                for half in range(2):
                    def _tpo(e, tt=tt, bsel=bsel, half=half):
                        r = None
                        for k in range(4 * half, 4 * half + 4):
                            b = bsel[k // 4]
                            r = e.transpose(PS[b][:, (k % 4) * 128:(k % 4) * 128 + 128],
                                            xr(k)[:, tt * 128:(tt + 1) * 128], IDF[:])
                        return r
                    wvh = [(SV, max(t_ready[4 * half:4 * half + 4]))] + [
                        g for g in ([bank_gate[b] for b in bsel[half:half + 1]]) if g]
                    pl.add("tensor", _tpo, waits=wvh, inc=ST)
                t_tp = ST.n
                wv2 = [(ST, t_tp)]
                if gateO[par]:
                    wv2.append(gateO[par])

                def _evo(e, par=par, bsel=bsel):
                    r = None
                    for kk in range(2):
                        for k in range(4):
                            r = e.tensor_copy(otok[par][:, (kk * 4 + k) * 128:(kk * 4 + k + 1) * 128],
                                              PS[bsel[kk]][:, k * 128:k * 128 + 128])
                    return r
                pl.add("vector", _evo, waits=wv2, inc=SV)
                t_ev = SV.n
                for b in bsel:
                    bank_gate[b] = (SV, t_ev)
                pl.add("scalar", lambda e, tt=tt, par=par: e.dma_start(
                    out=d_ox[tt * P:(tt + 1) * P, :], in_=otok[par][:]),
                    waits=[(SV, t_ev)], inc=SSD, inck=16)
                n_st += 1
                gateO[par] = (SSD, 16 * n_st)
            pl.add("scalar", lambda e: e.nop(), waits=[(SSD, 16 * n_st)])

        def plan_dbg_dump(src, t_ready):
            tmax = max(t_ready) if isinstance(t_ready, list) else t_ready
            pl.add("scalar", lambda e: e.dma_start(out=d_dbg, in_=src),
                   waits=[(SV, tmax), (ST, ST.n), (SS, SS.n)], inc=SSD, inck=16)
            pl.add("scalar", lambda e: e.nop(), waits=[(SSD, SSD.n)])

        ln_sq_gate[0] = [(ST, 0)]
        ln_xr_gate[0] = lambda k, b: [(SV, max(t_xtile[4 * b:4 * b + 4]))]
        if dbg == "xt":
            plan_dbg_dump(XR[:], t_xr)
        elif dbg == "shared":
            plan_layernorm()
            ticks, _abg = plan_shared(max(t_xn))
            plan_dbg_dump(XR[:], ticks)
        elif dbg == "ln1":
            plan_layernorm()
            pl.add("vector", lambda e: e.tensor_copy(AR2[:, 0:CT * T], XN[:]),
                   waits=[(SV, max(t_xn))], inc=SV)
            plan_dbg_dump(AR2[:, 0:CT * T], SV.n)
        else:
            ticks = plan_ffn(t_xr)
            set_ag()
            if dbg == "ffn1":
                plan_dbg_dump(XR[:], ticks)
            else:
                ticks = plan_attn(ticks)
                set_ag()
                if dbg == "attn":
                    plan_dbg_dump(XR[:], ticks)
                else:
                    ticks = plan_ffn(ticks)
                    set_ag()
                    if dbg == "ffn2":
                        plan_dbg_dump(XR[:], ticks)
                    else:
                        plan_out(ticks)

        with nc.Block() as blk:
            _emit(pl, blk)

    return nc


_cache = {}


def _get_nc(dbg=None):
    if dbg not in _cache:
        _cache[dbg] = build_program(dbg)
    return _cache[dbg]


def make_in_maps(inputs):
    x = np.asarray(inputs["L0_attn_x"])
    td = np.asarray(inputs["L0_attn_time_decay"])
    tf = np.asarray(inputs["L0_attn_time_first"])
    td2d = np.ascontiguousarray(td.reshape(CT, P).T)
    tf2d = np.ascontiguousarray(tf.reshape(CT, P).T)
    iden = np.eye(P, dtype=np.float32)
    shared = {
        "w_val": np.ascontiguousarray(inputs["L0_attn_value_w"], dtype=np.float32),
        "w_rec": np.ascontiguousarray(inputs["L0_attn_recep_w"], dtype=np.float32),
        "w_out": np.ascontiguousarray(inputs["L0_attn_out_w"], dtype=np.float32),
        "w_sw1": np.ascontiguousarray(inputs["L0_ffn_sw1"], dtype=np.float32),
        "w_sw3": np.ascontiguousarray(inputs["L0_ffn_sw3"], dtype=np.float32),
        "w_sw2": np.ascontiguousarray(inputs["L0_ffn_sw2"], dtype=np.float32),
        "w_ew1": np.ascontiguousarray(inputs["L0_ffn_ew1"], dtype=np.float32),
        "w_ew2": np.ascontiguousarray(inputs["L0_ffn_ew2"], dtype=np.float32),
        "w_router": np.ascontiguousarray(inputs["L0_ffn_router_w"], dtype=np.float32),
        "b_router": np.ascontiguousarray(
            np.asarray(inputs["L0_ffn_router_bias"], dtype=np.float32).reshape(E, 1)),
        "td2d": td2d, "tf2d": tf2d, "iden": iden,
        "oh8": np.ascontiguousarray(
            np.kron(np.eye(E, dtype=np.float32), np.ones((1, P), np.float32))),
    }
    in_maps = []
    for core in range(NCORES):
        b, q = core // 4, core % 4
        mask = np.zeros(NCORES, np.float32)
        expo = np.zeros(NCORES, np.float32)
        for qp in range(q):
            mask[b * 4 + qp] = 1.0
            expo[b * 4 + qp] = float(q - 1 - qp)
        im = dict(shared)
        im["xs"] = np.ascontiguousarray(x[b, q * T:(q + 1) * T, :], dtype=np.float32)
        im["mask2d"] = np.ascontiguousarray(np.tile(mask, (P, 1)))
        im["expo2d"] = np.ascontiguousarray(np.tile(expo, (P, 1)))
        in_maps.append(im)
    return in_maps


def kernel(**inputs):
    nc = _get_nc(None)
    in_maps = make_in_maps(inputs)
    res = run_bass_kernel_spmd(nc, in_maps, core_ids=list(range(NCORES)))
    out = np.empty((2, 4 * T, C), np.float32)
    for core in range(NCORES):
        b, q = core // 4, core % 4
        out[b, q * T:(q + 1) * T, :] = res.results[core]["out_x"]
    return out
